# version 24
# baseline (speedup 1.0000x reference)
import math
import numpy as np
import ml_dtypes
from contextlib import ExitStack

import concourse.bass as bass
import concourse.mybir as mybir
from concourse.bass_utils import run_bass_kernel_spmd

F32 = mybir.dt.float32
BF16 = mybir.dt.bfloat16
I32 = mybir.dt.int32
AF = mybir.ActivationFunctionType
ALU = mybir.AluOpType
AX = mybir.AxisListType

H, DH, BLK, NT, D, HH, R = 16, 128, 256, 2048, 2048, 8192, 16
NB = NT // BLK
KD = D // 128
KH = HH // 128
NTC = NT // 128
NG = 2
TG = NT // NG
NS = 8
SC = 8
EPS_LN, EPS_RMS = 1e-5, 1e-6

BF = ml_dtypes.bfloat16


def _slopes():
    start = 2.0 ** (-(2.0 ** -(math.log2(H) - 3)))
    return np.array([start * start ** i for i in range(H)], dtype=np.float64)


def _consts():
    s = _slopes()
    n = np.arange(128)[:, None]
    m = np.arange(256)[None, :]
    maskc = np.zeros((128, H, 384), dtype=np.float32)
    qdbc = np.zeros((128, H, 256), dtype=np.float32)
    kdec = np.zeros((128, H, 2), dtype=np.float32)
    for h in range(H):
        mm_ = np.exp(-s[h] * (m - n)) * (m >= n)
        maskc[:, h, 0:256] = mm_.astype(np.float32)
        maskc[:, h, 256:384] = mm_[:, 0:128].astype(np.float32)
        qdbc[:, h, :] = np.exp(-s[h] * (np.arange(256) + 1.0))[None, :].astype(np.float32)
        for c in range(2):
            kdec[:, h, c] = np.exp(-s[h] * (255.0 - (c * 128 + np.arange(128)))).astype(np.float32)
    bdec = [float(np.exp(-s[h] * BLK)) for h in range(H)]
    return maskc, qdbc.astype(BF), kdec, bdec


class _ES:

    def __init__(self):
        self.q = {k: [] for k in "tvsgy"}

    def add(self, ek, fn):
        self.q[ek].append(fn)

    def run(self, nc):
        def mk(lst):
            def body(e):
                for f in lst:
                    f(e)
            return body
        with nc.Block() as blk:
            if self.q["g"]:
                blk.gpsimd(mk(self.q["g"]))
            if self.q["y"]:
                blk.sync(mk(self.q["y"]))
            if self.q["t"]:
                blk.tensor(mk(self.q["t"]))
            if self.q["v"]:
                blk.vector(mk(self.q["v"]))
            if self.q["s"]:
                blk.scalar(mk(self.q["s"]))
        self.q = {k: [] for k in "tvsgy"}


def build_nc(dbg=False):
    nc = bass.Bass("TRN2", target_bir_lowering=False)

    x_d = nc.dram_tensor("x", [NT, D], F32, kind="ExternalInput").ap()
    wqkv_d = nc.dram_tensor("wqkv", [H, 128, KD, 384], F32, kind="ExternalInput").ap()
    wg_d = nc.dram_tensor("wg", [H, 128, KD, 128], F32, kind="ExternalInput").ap()
    wout_d = nc.dram_tensor("wout", [4, 128, KD, 512], F32, kind="ExternalInput").ap()
    nw_d = nc.dram_tensor("nw", [128, H], F32, kind="ExternalInput").ap()
    wupq_d = nc.dram_tensor("wupq", [KH, 128, KD, 128], I32, kind="ExternalInput").ap()
    wups_d = nc.dram_tensor("wups", [KH, 128, KD, 128], F32, kind="ExternalInput").ap()
    bup_d = nc.dram_tensor("bup", [128, KH], F32, kind="ExternalInput").ap()
    laA_d = nc.dram_tensor("laA", [128, KD, R], F32, kind="ExternalInput").ap()
    laB_d = nc.dram_tensor("laB", [R, HH], F32, kind="ExternalInput").ap()
    wdnq_d = nc.dram_tensor("wdnq", [KH, 128, D], I32, kind="ExternalInput").ap()
    wdns_d = nc.dram_tensor("wdns", [KH, 128, D], F32, kind="ExternalInput").ap()
    bdn_d = nc.dram_tensor("bdn", [1, D], F32, kind="ExternalInput").ap()
    ldA_d = nc.dram_tensor("ldA", [128, KH, R], F32, kind="ExternalInput").ap()
    ldB_d = nc.dram_tensor("ldB", [R, D], F32, kind="ExternalInput").ap()
    out_d = nc.dram_tensor("out", [NT, D], F32, kind="ExternalOutput").ap()

    attn_scr = nc.dram_tensor("attn_scr", [NT, D], BF16, kind="Internal").ap()
    xnT_scr = nc.dram_tensor("xnT_scr", [NTC, 128, KD, 128], BF16, kind="Internal").ap()
    scr_r = nc.dram_tensor("scr_r", [16, 128], F32, kind="Internal").ap()

    maskc_np, qdbc_np, kdec_np, bdec = _consts()
    maskc_d = nc.inline_tensor(maskc_np.astype(BF), "c_maskc")
    qdbc_d = nc.inline_tensor(qdbc_np, "c_qdbc")
    kdec_d = nc.inline_tensor(kdec_np, "c_kdec")
    identb_d = nc.inline_tensor(np.eye(128, dtype=BF), "c_identb")
    identf_d = nc.inline_tensor(np.eye(128, dtype=np.float32), "c_identf")

    if dbg:
        dbg_xnT = nc.dram_tensor("dbg_xnT", [128, KD, NT], F32, kind="ExternalOutput").ap()
        dbg_OT = nc.dram_tensor("dbg_OT", [128, H, NT], F32, kind="ExternalOutput").ap()
        dbg_og = nc.dram_tensor("dbg_og", [128, H, NT], F32, kind="ExternalOutput").ap()
        dbg_attn = nc.dram_tensor("dbg_attn", [NT, D], F32, kind="ExternalOutput").ap()
        dbg_r = nc.dram_tensor("dbg_r", [128, 48], F32, kind="ExternalOutput").ap()
        dbg_x2 = nc.dram_tensor("dbg_x2", [128, SC, TG], F32, kind="ExternalOutput").ap()

    cnt = {"pe": 0, "dv": 0, "ac": 0}
    NLANES = {"wl": 8, "hw": 4, "out": 4}

    with ExitStack() as S0:
        E0 = S0.enter_context
        sem = {
            "pe": E0(nc.semaphore("s_pe")), "dv": E0(nc.semaphore("s_dv")),
            "ac": E0(nc.semaphore("s_ac")),
        }
        for _cls, _nl in NLANES.items():
            for _i in range(_nl):
                _k = f"{_cls}{_i}"
                sem[_k] = E0(nc.semaphore("s_" + _k))
                cnt[_k] = 0
        dma_idx = {"wl": 0, "hw": 0, "out": 0}
        B = _ES()

        def wt(ek, sk, val):
            if val:
                B.add(ek, lambda e, s=sem[sk], v=val: e.wait_ge(s, v))

        def wtt(ek, tok):
            if tok:
                wt(ek, tok[0], tok[1])

        def wts(ek, toks):
            for t in toks:
                wtt(ek, t)

        def fence(cls):
            return [(f"{cls}{i}", cnt[f"{cls}{i}"]) for i in range(NLANES[cls])
                    if cnt[f"{cls}{i}"]]

        def op(ek, fn, inc=None, n=1):
            if inc is None:
                B.add(ek, fn)
                return None
            if inc in NLANES:
                lane = dma_idx[inc] % NLANES[inc]
                dma_idx[inc] += 1
                key = f"{inc}{lane}"
                cnt[key] += 16
                tok = (key, cnt[key])
                B.add(ek, lambda e, fn=fn, s=sem[key]: fn(e).then_inc(s, 16))
                return tok
            cnt[inc] += n
            tgt = cnt[inc]
            B.add(ek, lambda e, fn=fn, s=sem[inc], n=n: fn(e).then_inc(s, n))
            return tgt

        epsb = E0(nc.sbuf_tensor("epsb", [128, 2], F32))
        identb = E0(nc.sbuf_tensor("identb", [128, 128], BF16))
        ones_col = E0(nc.sbuf_tensor("ones_col", [128, 1], BF16))
        ones1b = E0(nc.sbuf_tensor("ones1b", [1, 128], BF16))

        with ExitStack() as S1:
            E1 = S1.enter_context
            xnT = E1(nc.sbuf_tensor("xnT", [128, KD, NT], BF16))
            OT = E1(nc.sbuf_tensor("OT", [128, H, NT], BF16))
            ps_rms = E1(nc.psum_tensor("ps_rms", [128, 16], F32))
            pb = [E1(nc.psum_tensor(f"pb{i}", [128, 512], F32)) for i in range(4)]
            TR = [E1(nc.psum_tensor(f"ptr{i}", [128, 512], BF16)) for i in range(2)]
            psK = E1(nc.psum_tensor("psK", [128, 256], F32))

            with ExitStack() as SP:
                E = SP.enter_context
                xt = [E(nc.sbuf_tensor(f"xt{i}", [128, D], F32)) for i in range(4)]
                xsq = [E(nc.sbuf_tensor(f"xsq{i}", [128, D], BF16)) for i in range(4)]
                xn_tok = [E(nc.sbuf_tensor(f"xntok{i}", [128, D], BF16)) for i in range(4)]
                st = E(nc.sbuf_tensor("lnst", [128, 4, 8], F32))

                idb_tok = op("y", lambda e: e.dma_start(identb[:], identb_d.ap()), "hw", 16)
                op("v", lambda e: e.memset(ones_col[:], 1.0), "dv")
                op("v", lambda e: e.memset(ones1b[:], 1.0), "dv")
                op("v", lambda e: e.memset(epsb[:, 0:1], EPS_LN), "dv")
                epsv = op("v", lambda e: e.memset(epsb[:, 1:2], EPS_RMS), "dv")

                apply_inc = [0] * NTC
                xnw_toks = [None] * NTC
                sqd_inc = [0] * NTC
                e2_arr = [0] * NTC
                trev_inc = [0] * (NTC * 4)
                trmm_inc = [0] * (NTC * 4)
                xld = [None] * NTC
                for tp_ in range(NTC // 4):
                    tcs = tuple(4 * tp_ + j_ for j_ in range(4))
                    e1 = [0] * 4
                    e2 = [0] * 4
                    e3 = [0] * 4
                    e4 = [0] * 4
                    e5 = [0] * 4
                    e6 = [0] * 4
                    e7 = [0] * 4
                    for i, tc in enumerate(tcs):
                        ii = tc % 4
                        if tc >= 4:
                            wt("y", "dv", apply_inc[tc - 4])
                            wt("y", "ac", sqd_inc[tc - 4])
                        xld[tc] = op("y", lambda e, tc=tc, ii=ii: e.dma_start(
                            xt[ii][:], x_d[tc * 128:(tc + 1) * 128, :]), "hw", 16)
                    for i, tc in enumerate(tcs):
                        ii = tc % 4
                        wtt("v", xld[tc])
                        e1[i] = op("v", lambda e, i=i, ii=ii: e.reduce_sum(st[:, i, 0:1], xt[ii][:], AX.X), "dv")
                    for i, tc in enumerate(tcs):
                        ii = tc % 4
                        wtt("s", xld[tc])
                        if tc >= 4:
                            wt("s", "dv", e2_arr[tc - 4])
                        sqd_inc[tc] = op("s", lambda e, i=i, ii=ii: e.activation(
                            xsq[i][:], xt[ii][:], AF.Square), "ac")
                    for i, tc in enumerate(tcs):
                        wt("v", "ac", sqd_inc[tc])
                        e2[i] = op("v", lambda e, i=i: e.reduce_sum(st[:, i, 1:2], xsq[i][:], AX.X), "dv")
                        e2_arr[tc] = e2[i]
                    for i, tc in enumerate(tcs):
                        wt("v", "dv", e2[i])
                        e3[i] = op("v", lambda e, i=i: e.tensor_scalar(
                            st[:, i, 2:4], st[:, i, 0:2], 1.0 / D, None, ALU.mult), "dv")
                    for i, tc in enumerate(tcs):
                        wt("v", "dv", e3[i])
                        e4[i] = op("v", lambda e, i=i: e.tensor_tensor(
                            st[:, i, 4:5], st[:, i, 2:3], st[:, i, 2:3], ALU.mult), "dv")
                    for i, tc in enumerate(tcs):
                        wt("v", "dv", e4[i])
                        e5[i] = op("v", lambda e, i=i: e.tensor_tensor(
                            st[:, i, 5:6], st[:, i, 3:4], st[:, i, 4:5], ALU.subtract), "dv")
                    for i, tc in enumerate(tcs):
                        wt("s", "dv", e5[i])
                        if tp_ == 0 and i == 0:
                            wt("s", "dv", epsv)
                        e6[i] = op("s", lambda e, i=i: e.activation(
                            st[:, i, 6:7], st[:, i, 5:6], AF.Sqrt, bias=epsb[:, 0:1]), "ac")
                    for i, tc in enumerate(tcs):
                        wt("v", "ac", e6[i])
                        e7[i] = op("v", lambda e, i=i: e.reciprocal(st[:, i, 7:8], st[:, i, 6:7]), "dv")
                    for i, tc in enumerate(tcs):
                        wt("v", "dv", e7[i])
                        if tc >= 4:
                            wt("v", "pe", trmm_inc[(tc - 4) * 4 + 3])
                        apply_inc[tc] = op("v", lambda e, i=i, ii=tc % 4: e.tensor_scalar(
                            xn_tok[i][:], xt[ii][:], st[:, i, 2:3], st[:, i, 7:8],
                            ALU.subtract, ALU.mult), "dv")
                    for i, tc in enumerate(tcs):
                        for grp in range(4):
                            gidx = tc * 4 + grp
                            tp2 = gidx % 2
                            wt("t", "dv", apply_inc[tc])
                            if gidx >= 2:
                                wt("t", "dv", trev_inc[gidx - 2])
                            if gidx == 0:
                                wtt("t", idb_tok)
                            for j in range(4):
                                k = grp * 4 + j
                                inc = "pe" if j == 3 else None
                                r = op("t", lambda e, tp2=tp2, j=j, k=k, i=i: e.transpose(
                                    TR[tp2][:, j * 128:(j + 1) * 128],
                                    xn_tok[i][:, k * 128:(k + 1) * 128], identb[:]), inc)
                                if j == 3:
                                    trmm_inc[gidx] = r
                            wt("v", "pe", trmm_inc[gidx])
                            trev_inc[gidx] = op("v", lambda e, tp2=tp2, tc=tc, grp=grp: e.tensor_copy(
                                xnT[:, grp * 4:(grp + 1) * 4, tc * 128:(tc + 1) * 128],
                                TR[tp2][:].rearrange("p (j c) -> p j c", j=4)), "dv")
                        wt("g", "dv", trev_inc[tc * 4 + 3])
                        xnw_toks[tc] = op("g", lambda e, tc=tc: e.dma_start(
                            xnT_scr[tc], xnT[:, :, tc * 128:(tc + 1) * 128]), "wl", 16)
                B.run(nc)

            if dbg:
                with ExitStack() as SP:
                    op("g", lambda e: e.dma_start(dbg_xnT, xnT[:]), "wl", 16)
                    wts("g", fence("wl"))
                    B.run(nc)

            with ExitStack() as SP:
                E = SP.enter_context
                whb = [E(nc.sbuf_tensor(f"whb{i}", [128, KD, 384], BF16)) for i in range(2)]
                qkv = [E(nc.sbuf_tensor(f"qkv{i}", [128, 3, NT], BF16)) for i in range(2)]
                maskp = [E(nc.sbuf_tensor(f"maskp{i}", [128, 2, 384], BF16)) for i in range(2)]
                qdbp = [E(nc.sbuf_tensor(f"qdbp{i}", [128, 2, 256], BF16)) for i in range(2)]
                kdec_sb = E(nc.sbuf_tensor("kdec", [128, H, 2], F32))
                Sd = [E(nc.sbuf_tensor(f"Sd{i}", [128, 384], BF16)) for i in range(2)]
                vtok = [E(nc.sbuf_tensor(f"vtok{i}", [128, 2, 2, 128], BF16)) for i in range(2)]
                kdt = [E(nc.sbuf_tensor(f"kdt{i}", [128, 2, 2, 128], BF16)) for i in range(2)]
                qd = [E(nc.sbuf_tensor(f"qd{i}", [128, 256], BF16)) for i in range(2)]
                sqb = E(nc.sbuf_tensor("sqb", [128, 2, NB, 256], BF16))
                kv2 = [E(nc.sbuf_tensor(f"kv{i}", [128, 128], F32)) for i in range(2)]
                kvbf2 = [E(nc.sbuf_tensor(f"kvbf{i}", [128, 128], BF16)) for i in range(2)]
                wgh = [E(nc.sbuf_tensor("wgh0", [128, KD, 2, 128], BF16))] * 2
                g_sb = [E(nc.sbuf_tensor(f"gsb{i}", [128, 512], BF16)) for i in range(2)]
                ogt = g_sb
                nw_sb = E(nc.sbuf_tensor("nwsb", [128, H], F32))

                kdl = op("y", lambda e: e.dma_start(kdec_sb[:], kdec_d.ap()), "hw", 16)
                nwl = op("y", lambda e: e.dma_start(nw_sb[:], nw_d), "hw", 16)

                whb_pe_free = [0, 0]
                qkv_pe_last = [0, 0]
                qkv_dv_last = [0, 0]
                silu_done = [0, 0]
                bank_free = {i: ("ac", 0) for i in range(4)}
                trbank_free = [None, None]
                sd_pe_free = [0, 0]
                vk_pe_free = [[0, 0], [0, 0]]
                psK_free = [0, 0]
                pbO_free = [("dv", 0), ("dv", 0)]
                pbS_free = [("dv", 0), ("dv", 0)]
                kvbf_src = [("dv", 0), ("dv", 0)]
                last_kvc = [0, 0]
                mask_dv_free = [0, 0]
                pending_rms = None
                rms_pe_last = 0
                first_rms = [True]
                wgh_pe_free = [0, 0]
                ogpair_dv_free = [0, 0]
                og_dv_last = 0

                def emit_rms(sq_last):
                    wt("t", "dv", sq_last)
                    r = 0
                    for p_ in range(2):
                        for b in range(NB):
                            for mc in range(2):
                                fr = first_rms[0]
                                first_rms[0] = False
                                inc = "pe" if (p_ == 1 and b == NB - 1 and mc == 1) else None
                                r = op("t", lambda e, p_=p_, b=b, mc=mc, fr=fr: e.matmul(
                                    ps_rms[:, 2 * b + mc:2 * b + mc + 1],
                                    sqb[:, p_, b, mc * 128:(mc + 1) * 128], ones_col[:],
                                    start=fr, stop=True), inc) or r
                    return r

                for hp in range(H // 2):
                    pp = hp % 2
                    heads = (2 * hp, 2 * hp + 1)
                    wt("y", "dv", mask_dv_free[pp])
                    mtok = op("y", lambda e, hp=hp, pp=pp: e.dma_start(
                        maskp[pp][:], maskc_d.ap()[:, 2 * hp:2 * hp + 2, :]), "hw", 16)
                    qtok = op("y", lambda e, hp=hp, pp=pp: e.dma_start(
                        qdbp[pp][:], qdbc_d.ap()[:, 2 * hp:2 * hp + 2, :]), "hw", 16)
                    for p_ in range(2):
                        h = heads[p_]
                        wt("g", "pe", whb_pe_free[p_])
                        wq = op("g", lambda e, h=h, p_=p_: e.dma_start(
                            whb[p_][:], wqkv_d[h]), "wl", 16)
                        for f in range(3):
                            for tb in range(4):
                                rot = (f * 4 + tb) % 4
                                wtt("t", wq)
                                bk, bv = bank_free[rot]
                                wt("t", bk, bv)
                                if rot < 2:
                                    bk2, bv2 = pbS_free[rot]
                                    wt("t", bk2, bv2)
                                else:
                                    bk2, bv2 = pbO_free[rot - 2]
                                    wt("t", bk2, bv2)
                                mmv = None
                                for k in range(KD):
                                    inc = "pe" if k == KD - 1 else None
                                    mmv = op("t", lambda e, p_=p_, k=k, f=f, tb=tb, rot=rot: e.matmul(
                                        pb[rot][:], whb[p_][:, k, f * 128:(f + 1) * 128],
                                        xnT[:, k, tb * 512:(tb + 1) * 512],
                                        start=(k == 0), stop=(k == KD - 1)), inc) or mmv
                                whb_pe_free[p_] = mmv
                                wt("s", "pe", mmv)
                                if hp > 0:
                                    wt("s", "pe", qkv_pe_last[p_])
                                    wt("s", "dv", qkv_dv_last[p_])
                                sv = op("s", lambda e, p_=p_, f=f, tb=tb, rot=rot: e.activation(
                                    qkv[p_][:, f, tb * 512:(tb + 1) * 512], pb[rot][:], AF.Silu), "ac")
                                bank_free[rot] = ("ac", sv)
                                silu_done[p_] = sv
                        wt("v", "dv", last_kvc[p_])
                        op("v", lambda e, p_=p_: e.memset(kv2[p_][:], 0.0), "dv")
                        kvi = op("v", lambda e, p_=p_: e.memset(kvbf2[p_][:], 0.0), "dv")
                        kvbf_src[p_] = ("dv", kvi)
                    if pending_rms is not None:
                        rms_pe_last = emit_rms(pending_rms)
                        pending_rms = None
                    oev = [[0] * NB, [0] * NB]
                    tr7 = [0, 0]
                    pending_kv = None

                    def emit_kvchain(p2, h2, blc2, o_mm2, kvd_mm2, b2):
                        wt("v", "pe", o_mm2)
                        oev[p2][b2] = op("v", lambda e, p2=p2, h2=h2, blc2=blc2: e.tensor_copy(
                            OT[:, h2, blc2:blc2 + 256], pb[2 + p2][:, 0:256]), "dv")
                        pbO_free[p2] = ("dv", oev[p2][b2])
                        wt("v", "pe", kvd_mm2)
                        psK_free[p2] = op("v", lambda e, p2=p2, h2=h2: e.scalar_tensor_tensor(
                            kv2[p2][:], kv2[p2][:], bdec[h2], psK[:, p2 * 128:(p2 + 1) * 128],
                            ALU.mult, ALU.add), "dv")
                        if b2 < NB - 1:
                            wt("v", "dv", psK_free[p2])
                            kvc = op("v", lambda e, p2=p2: e.tensor_copy(
                                kvbf2[p2][:], kv2[p2][:]), "dv")
                            kvbf_src[p2] = ("dv", kvc)
                            last_kvc[p2] = kvc
                    for b in range(NB):
                        blc = b * 256
                        for p_ in range(2):
                            h = heads[p_]
                            wt("t", "ac", silu_done[p_])
                            bk, bv = pbS_free[p_]
                            wt("t", bk, bv)
                            op("t", lambda e, p_=p_, blc=blc: e.matmul(
                                pb[p_][:, 0:256], qkv[p_][:, 1, blc:blc + 128],
                                qkv[p_][:, 0, blc:blc + 256], start=True, stop=False))
                            s_mm = op("t", lambda e, p_=p_, blc=blc: e.matmul(
                                pb[p_][:, 256:384], qkv[p_][:, 1, blc + 128:blc + 256],
                                qkv[p_][:, 0, blc + 128:blc + 256], start=False, stop=True), "pe")
                            if trbank_free[p_]:
                                wt("t", trbank_free[p_][0], trbank_free[p_][1])
                            tr_half = 0
                            for j, (fch, off) in enumerate(((2, 0), (2, 128), (1, 0), (1, 128))):
                                inc = "pe" if j in (1, 3) else None
                                r = op("t", lambda e, p_=p_, j=j, fch=fch, off=off, blc=blc: e.transpose(
                                    TR[p_][:, j * 128:(j + 1) * 128],
                                    qkv[p_][:, fch, blc + off:blc + off + 128], identb[:]), inc)
                                if j == 1:
                                    tr_half = r
                                if j == 3:
                                    tr_mm = r
                            tr7[p_] = tr_mm
                            if b == 0:
                                wtt("v", mtok)
                                wtt("v", qtok)
                                if hp == 0 and p_ == 0:
                                    wtt("v", kdl)
                            wt("v", "pe", s_mm)
                            wt("v", "pe", sd_pe_free[p_])
                            sd_done = op("v", lambda e, p_=p_, pp=pp: e.tensor_tensor(
                                Sd[p_][:, 0:384], pb[p_][:, 0:384], maskp[pp][:, p_, :], ALU.mult), "dv")
                            pbS_free[p_] = ("dv", sd_done)
                            qd_done = op("v", lambda e, p_=p_, pp=pp, blc=blc: e.tensor_tensor(
                                qd[p_][:], qkv[p_][:, 0, blc:blc + 256],
                                qdbp[pp][:, p_, :], ALU.mult), "dv")
                            qkv_dv_last[p_] = qd_done
                            mask_dv_free[pp] = qd_done
                            if pending_kv is not None:
                                emit_kvchain(*pending_kv)
                                pending_kv = None
                            sp = b % 2
                            wt("s", "pe", tr_half)
                            wt("s", "pe", vk_pe_free[p_][sp])
                            if b == 0 and hp == 0 and p_ == 0:
                                wtt("s", kdl)
                            op("s", lambda e, p_=p_, sp=sp: e.activation(vtok[p_][:, sp, 0, :], TR[p_][:, 0:128], AF.Copy), "ac")
                            op("s", lambda e, p_=p_, sp=sp: e.activation(vtok[p_][:, sp, 1, :], TR[p_][:, 128:256], AF.Copy), "ac")
                            wt("s", "pe", tr_mm)
                            op("s", lambda e, p_=p_, h=h, sp=sp: e.activation(
                                kdt[p_][:, sp, 0, :], TR[p_][:, 256:384], AF.Copy, scale=kdec_sb[:, h, 0:1]), "ac")
                            vk_done = op("s", lambda e, p_=p_, h=h, sp=sp: e.activation(
                                kdt[p_][:, sp, 1, :], TR[p_][:, 384:512], AF.Copy, scale=kdec_sb[:, h, 1:2]), "ac")
                            trbank_free[p_] = ("ac", vk_done)
                            wt("t", "dv", qd_done)
                            wt("t", "ac", vk_done)
                            op("t", lambda e, p_=p_: e.matmul(
                                pb[2 + p_][:, 0:256], kvbf2[p_][:], qd[p_][:], start=True, stop=False))
                            op("t", lambda e, p_=p_, sp=sp: e.matmul(
                                pb[2 + p_][:, 0:256], vtok[p_][:, sp, 0, :], Sd[p_][:, 0:256],
                                start=False, stop=False))
                            o_mm = op("t", lambda e, p_=p_, sp=sp: e.matmul(
                                pb[2 + p_][:, 128:256], vtok[p_][:, sp, 1, :], Sd[p_][:, 256:384],
                                start=False, stop=True), "pe")
                            sd_pe_free[p_] = o_mm
                            op("t", lambda e, p_=p_, sp=sp: e.matmul(
                                psK[:, p_ * 128:(p_ + 1) * 128],
                                kdt[p_][:, sp, 0, :], vtok[p_][:, sp, 0, :], start=(p_ == 0), stop=False))
                            kvd_mm = op("t", lambda e, p_=p_, sp=sp: e.matmul(
                                psK[:, p_ * 128:(p_ + 1) * 128],
                                kdt[p_][:, sp, 1, :], vtok[p_][:, sp, 1, :], start=False, stop=True), "pe")
                            vk_pe_free[p_][sp] = kvd_mm
                            pending_kv = (p_, h, blc, o_mm, kvd_mm, b)
                    if pending_kv is not None:
                        emit_kvchain(*pending_kv)
                        pending_kv = None
                    qkv_pe_last = [tr7[0], tr7[1]]
                    sq_last = 0
                    for p_ in range(2):
                        for b in range(NB):
                            wt("v", "dv", oev[p_][b])
                            if p_ == 0 and b == 0:
                                wt("v", "pe", rms_pe_last)
                            sq_last = op("v", lambda e, p_=p_, b=b, h=heads[p_]: e.tensor_tensor(
                                sqb[:, p_, b, :], OT[:, h, b * 256:b * 256 + 256],
                                OT[:, h, b * 256:b * 256 + 256], ALU.mult), "dv")
                    pending_rms = sq_last
                    wgl = [None, None]
                    for p_ in range(2):
                        h = heads[p_]
                        wt("g", "pe", max(wgh_pe_free))
                        wgl[p_] = op("g", lambda e, h=h, p_=p_, pp=pp: e.dma_start(
                            wgh[pp][:, :, p_, :], wg_d[h]), "wl", 16)
                    for p_ in range(2):
                        h = heads[p_]
                        for tb in range(4):
                            rot = (p_ * 4 + tb) % 4
                            r2 = (p_ * 4 + tb) % 2
                            wtt("t", wgl[p_])
                            bk, bv = bank_free[rot]
                            wt("t", bk, bv)
                            if rot < 2:
                                bk2, bv2 = pbS_free[rot]
                                wt("t", bk2, bv2)
                            else:
                                bk2, bv2 = pbO_free[rot - 2]
                                wt("t", bk2, bv2)
                            mmv = None
                            for k in range(KD):
                                inc = "pe" if k == KD - 1 else None
                                mmv = op("t", lambda e, pp=pp, p_=p_, k=k, tb=tb, rot=rot: e.matmul(
                                    pb[rot][:], wgh[pp][:, k, p_, :],
                                    xnT[:, k, tb * 512:(tb + 1) * 512],
                                    start=(k == 0), stop=(k == KD - 1)), inc) or mmv
                            wgh_pe_free[pp] = mmv
                            wt("s", "pe", mmv)
                            wt("s", "dv", ogpair_dv_free[r2])
                            g_ev = op("s", lambda e, rot=rot, r2=r2: e.activation(
                                g_sb[r2][:], pb[rot][:], AF.Sigmoid), "ac")
                            bank_free[rot] = ("ac", g_ev)
                            wt("v", "ac", g_ev)
                            if hp == 0 and p_ == 0 and tb == 0:
                                wtt("v", nwl)
                            t1 = op("v", lambda e, r2=r2, h=h: e.tensor_scalar(
                                g_sb[r2][:], g_sb[r2][:], nw_sb[:, h:h + 1], None, ALU.mult), "dv")
                            wt("v", "dv", t1)
                            wt("v", "dv", pending_rms)
                            og_dv_last = op("v", lambda e, r2=r2, h=h, tb=tb: e.tensor_tensor(
                                OT[:, h, tb * 512:(tb + 1) * 512],
                                OT[:, h, tb * 512:(tb + 1) * 512], ogt[r2][:], ALU.mult), "dv")
                            ogpair_dv_free[r2] = og_dv_last
                rms_pe_last = emit_rms(pending_rms)
                B.run(nc)

            if dbg:
                with ExitStack() as SP:
                    op("g", lambda e: e.dma_start(dbg_OT, OT[:]), "wl", 16)
                    wts("g", fence("wl"))
                    B.run(nc)

            with ExitStack() as SP:
                E = SP.enter_context
                wstage = [E(nc.sbuf_tensor(f"wstg{i}", [128, KD, 512], BF16)) for i in range(2)]
                astage = [E(nc.sbuf_tensor(f"astg{i}", [128, 512], BF16)) for i in range(2)]
                rms_sb = E(nc.sbuf_tensor("rms", [128, 48], F32))
                bbc = E(nc.sbuf_tensor("bbc", [128, D], BF16))
                bdn_sb = E(nc.sbuf_tensor("bdnsb", [1, D], BF16))

                bdl = op("g", lambda e: e.dma_start(bdn_sb[:], bdn_d), "wl", 16)

                r1 = op("v", lambda e: e.tensor_scalar(
                    rms_sb[:, 0:16], ps_rms[:], 1.0 / D, None, ALU.mult), "dv")
                wt("s", "dv", r1)
                r2 = op("s", lambda e: e.activation(
                    rms_sb[:, 16:32], rms_sb[:, 0:16], AF.Sqrt, bias=epsb[:, 1:2]), "ac")
                wt("v", "ac", r2)
                r3 = op("v", lambda e: e.reciprocal(rms_sb[:, 32:48], rms_sb[:, 16:32]), "dv")
                bbc_ev = 0
                for j in range(4):
                    wtt("t", bdl)
                    wt("t", "dv", bbc_ev)
                    r8 = op("t", lambda e, j=j: e.matmul(
                        pb[0][:], ones1b[:], bdn_sb[0:1, j * 512:(j + 1) * 512],
                        start=True, stop=True), "pe")
                    wt("v", "pe", r8)
                    bbc_ev = op("v", lambda e, j=j: e.tensor_copy(
                        bbc[:, j * 512:(j + 1) * 512], pb[0][:]), "dv")
                if dbg:
                    wt("g", "dv", r3)
                    op("g", lambda e: e.dma_start(dbg_r, rms_sb[:]), "wl", 16)
                    op("g", lambda e: e.dma_start(dbg_og, OT[:]), "wl", 16)

                wstage_pe_free = [0, 0]
                bank_free = {i: ("dv", bbc_ev if i == 0 else 0) for i in range(4)}
                astage_free = [None, None]
                nit = 0
                for st4 in range(4):
                    p_ = st4 % 2
                    wt("g", "pe", wstage_pe_free[p_])
                    wo_ld = op("g", lambda e, st4=st4, p_=p_: e.dma_start(
                        wstage[p_][:], wout_d[st4]), "wl", 16)
                    for tc in range(NTC):
                        rot = nit % 4
                        r2p = nit % 2
                        wtt("t", wo_ld)
                        bk, bv = bank_free[rot]
                        wt("t", bk, bv)
                        w_mm = None
                        for k in range(KD):
                            inc = "pe" if k == KD - 1 else None
                            w_mm = op("t", lambda e, p_=p_, k=k, tc=tc, rot=rot: e.matmul(
                                pb[rot][:], OT[:, k, tc * 128:(tc + 1) * 128],
                                wstage[p_][:, k, :], start=(k == 0), stop=(k == KD - 1)),
                                inc) or w_mm
                        wstage_pe_free[p_] = w_mm
                        wt("v", "pe", w_mm)
                        wtt("v", astage_free[r2p])
                        if nit == 0:
                            wt("v", "dv", r3)
                            wt("v", "dv", bbc_ev)
                        a_ev = op("v", lambda e, rot=rot, r2p=r2p, tc=tc, st4=st4: e.scalar_tensor_tensor(
                            astage[r2p][:], pb[rot][:], rms_sb[:, 32 + tc:33 + tc],
                            bbc[:, st4 * 512:(st4 + 1) * 512], ALU.mult, ALU.add), "dv")
                        bank_free[rot] = ("dv", a_ev)
                        wt("y", "dv", a_ev)
                        astage_free[r2p] = op("y", lambda e, r2p=r2p, tc=tc, st4=st4: e.dma_start(
                            attn_scr[tc * 128:(tc + 1) * 128, st4 * 512:(st4 + 1) * 512],
                            astage[r2p][:]), "hw", 16)
                        nit += 1
                attn_wr_done = fence("hw")
                B.run(nc)

            if dbg:
                with ExitStack() as SP:
                    wts("g", attn_wr_done)
                    op("g", lambda e: e.dma_start(dbg_attn, attn_scr), "wl", 16)
                    wts("g", fence("wl"))
                    B.run(nc)

        with ExitStack() as SP:
            E = SP.enter_context
            ACC = E(nc.sbuf_tensor("ACC", [128, 8, D], F32))
            x2s = E(nc.sbuf_tensor("x2s", [128, SC, TG], BF16))
            xng = E(nc.sbuf_tensor("xng", [128, KD, TG], BF16))
            wdn = [[E(nc.sbuf_tensor(f"wdn{p}_{i}", [128, D], BF16)) for i in range(SC)]
                   for p in range(2)]
            s_stage = [E(nc.sbuf_tensor(f"sstage{i}", [128, D], BF16)) for i in range(2)]
            upq = E(nc.sbuf_tensor("upq", [128, KD, 128], I32))
            wupbf = [E(nc.sbuf_tensor(f"wupbf{i}", [128, KD, 128], BF16)) for i in range(2)]
            la_sb = E(nc.sbuf_tensor("lasb", [16, TG], BF16))
            laBs = E(nc.sbuf_tensor("laBs", [16, 1024], BF16))
            ldb = la_sb
            laA_sb = E(nc.sbuf_tensor("laAsb", [128, KD, R], BF16))
            ldA_sb = [E(nc.sbuf_tensor(f"ldAsb{i}", [128, SC, R], BF16)) for i in range(2)]
            Bd_sb = E(nc.sbuf_tensor("Bdsb", [16, 1024], BF16))
            bup_sb = E(nc.sbuf_tensor("bupsb", [128, KH], F32))
            ps_up = [E(nc.psum_tensor(f"psup{i}", [128, 512], F32)) for i in range(2)]
            pdown = [E(nc.psum_tensor(f"pdn{i}", [128, 1024], F32)) for i in range(2)]
            ps_ld = E(nc.psum_tensor("psld", [16, 1024], F32))

            wts("g", attn_wr_done)
            wts("y", attn_wr_done)
            wts("v", attn_wr_done)
            cl1 = op("g", lambda e: e.dma_start(laA_sb[:], laA_d), "wl", 16)
            cl4 = op("y", lambda e: e.dma_start(bup_sb[:], bup_d), "hw", 16)

            out_dma = [[None] * 8 for _ in range(NG)]
            pe_x2_free = 0
            wdn_free = [0, 0]
            dq_next = None
            pdown_free = [("dv", 0), ("dv", 0)]
            psup_free = [("ac", 0), ("ac", 0)]
            s_slot_user = [0, 0]
            s_use = 0
            upq_dv_free = [0, 0]
            wupbf_pe_free = [0, 0]
            psld_free = 0
            ldA_pe_free = [0, 0]
            laBs_pe_free = 0
            ld_mm_pe = 0
            ldb_pe_free = 0
            la_pe_free = 0

            dq_next = []
            for ci in range(SC):
                dq_next.append(op("g", lambda e, ci=ci: e.dma_start(
                    wdn[0][ci][:], wdnq_d[ci]), "wl", 16))

            xng_toks_next = None
            pre_us = {}
            for gi in range(NG):
                gcol = gi * TG
                if gi == 0:
                    xng_toks = [None] * 8
                    for tc8 in range(8):
                        wtt("y", xnw_toks[tc8])
                        xng_toks[tc8] = op("y", lambda e, tc8=tc8: e.dma_start(
                            xng[:, :, tc8 * 128:(tc8 + 1) * 128], xnT_scr[tc8]), "hw", 16)
                else:
                    xng_toks = xng_toks_next
                la_ev = 0
                for tb2 in range(2):
                    wtt("t", cl1)
                    wts("t", [t for t in xng_toks[tb2 * 4:(tb2 + 1) * 4] if t])
                    bk, bv = psup_free[tb2]
                    wt("t", bk, bv)
                    mmv = None
                    for k in range(KD):
                        inc = "pe" if k == KD - 1 else None
                        mmv = op("t", lambda e, k=k, tb2=tb2: e.matmul(
                            ps_up[tb2][0:16, :], laA_sb[:, k, :],
                            xng[:, k, tb2 * 512:(tb2 + 1) * 512],
                            start=(k == 0), stop=(k == KD - 1)), inc) or mmv
                    wt("v", "pe", mmv)
                    if gi > 0:
                        wt("v", "pe", la_pe_free)
                    la_ev = op("v", lambda e, tb2=tb2: e.tensor_copy(
                        la_sb[:, tb2 * 512:(tb2 + 1) * 512], ps_up[tb2][0:16, :]), "dv")
                    psup_free[tb2] = ("dv", la_ev)

                for s5 in range(NS):
                    S = gi * NS + s5
                    par = S % 2
                    dq_lds = dq_next
                    ds_toks = [None] * SC
                    ds_slots = [0] * SC
                    deq_dn_arr = [0] * SC
                    wt("g", "pe", laBs_pe_free)
                    lb_ld = op("g", lambda e, s5=s5: e.dma_start(
                        laBs[:], laB_d[:, s5 * 1024:(s5 + 1) * 1024]), "wl", 16)
                    sp_ = s5 % 2
                    wt("g", "pe", ldA_pe_free[sp_])
                    ldA_ld = op("g", lambda e, s5=s5, sp_=sp_: e.dma_start(
                        ldA_sb[sp_][:], ldA_d[:, s5 * SC:(s5 + 1) * SC, :]), "wl", 16)
                    relu_ev = 0
                    for ci in range(SC):
                        c = s5 * SC + ci
                        wp = ci % 2
                        if gi > 0 and s5 == 0 and ci in pre_us:
                            us_tok, slot = pre_us[ci]
                        else:
                            slot = s_use % 2
                            s_use += 1
                            wt("g", "dv", s_slot_user[slot])
                            us_tok = op("g", lambda e, c=c, slot=slot: e.dma_start(
                                s_stage[slot][:].rearrange("p (k c) -> p k c", k=KD),
                                wups_d[c]), "wl", 16)
                        KH2 = KD // 2
                        dqh = [0, 0]
                        for hk in range(2):
                            wt("y", "dv", upq_dv_free[hk])
                            uq_ld = op("y", lambda e, c=c, hk=hk, KH2=KH2: e.dma_start(
                                upq[:, hk * KH2:(hk + 1) * KH2, :],
                                wupq_d[c][:, hk * KH2:(hk + 1) * KH2, :]), "hw", 16)
                            wtt("v", uq_ld)
                            if hk == 0:
                                wtt("v", us_tok)
                                wt("v", "pe", wupbf_pe_free[wp])
                            dqh[hk] = op("v", lambda e, wp=wp, slot=slot, hk=hk, KH2=KH2: e.scalar_tensor_tensor(
                                wupbf[wp][:, hk * KH2:(hk + 1) * KH2, :],
                                upq[:, hk * KH2:(hk + 1) * KH2, :], -7.5,
                                s_stage[slot][:].rearrange("p (k c) -> p k c", k=KD)[:, hk * KH2:(hk + 1) * KH2, :],
                                ALU.add, ALU.mult), "dv")
                            upq_dv_free[hk] = dqh[hk]
                        dq = dqh[1]
                        s_slot_user[slot] = dq
                        djs = [ci - 1] if ci >= 1 else []
                        if ci == SC - 1:
                            djs += [SC - 1]
                        for dj in djs:
                            slotd = s_use % 2
                            s_use += 1
                            wt("g", "dv", s_slot_user[slotd])
                            ds_toks[dj] = op("g", lambda e, dj=dj, slotd=slotd, s5=s5: e.dma_start(
                                s_stage[slotd][:], wdns_d[s5 * SC + dj]), "wl", 16)
                            ds_slots[dj] = slotd
                            wtt("v", dq_lds[dj])
                            wtt("v", ds_toks[dj])
                            deq_dn_arr[dj] = op("v", lambda e, dj=dj, slotd=slotd, par=par: e.scalar_tensor_tensor(
                                wdn[par][dj][:], wdn[par][dj][:], -7.5, s_stage[slotd][:],
                                ALU.add, ALU.mult), "dv")
                            s_slot_user[slotd] = deq_dn_arr[dj]
                        for tb2 in range(2):
                            if tb2 == 0:
                                wt("t", "dv", dqh[0])
                            wtt("t", lb_ld)
                            wt("t", "dv", la_ev)
                            bk, bv = psup_free[tb2]
                            wt("t", bk, bv)
                            if pe_x2_free:
                                wt("s", "pe", pe_x2_free)
                            for k in range(KD):
                                if tb2 == 0 and k == KD // 2:
                                    wt("t", "dv", dqh[1])
                                op("t", lambda e, wp=wp, k=k, tb2=tb2: e.matmul(
                                    ps_up[tb2][:], wupbf[wp][:, k, :],
                                    xng[:, k, tb2 * 512:(tb2 + 1) * 512],
                                    start=(k == 0), stop=False))
                            upmm = op("t", lambda e, ci=ci, tb2=tb2: e.matmul(
                                ps_up[tb2][:], laBs[:, ci * 128:(ci + 1) * 128],
                                la_sb[:, tb2 * 512:(tb2 + 1) * 512],
                                start=False, stop=True), "pe")
                            wupbf_pe_free[wp] = upmm
                            laBs_pe_free = upmm
                            la_pe_free = upmm
                            wt("s", "pe", upmm)
                            if gi == 0 and s5 == 0 and ci == 0 and tb2 == 0:
                                wtt("s", cl4)
                            relu_ev = op("s", lambda e, ci=ci, tb2=tb2, c=c: e.activation(
                                x2s[:, ci, tb2 * 512:(tb2 + 1) * 512], ps_up[tb2][:],
                                AF.Relu, bias=bup_sb[:, c:c + 1]), "ac")
                            psup_free[tb2] = ("ac", relu_ev)
                    wt("t", "ac", relu_ev)
                    if s5 == 0:
                        wt("t", "dv", psld_free)
                        wtt("t", ldA_ld)
                    else:
                        wtt("t", ldA_ld)
                    for tb2 in range(2):
                        for ci in range(SC):
                            inc = "pe" if (ci == SC - 1 and tb2 == 1) else None
                            r = op("t", lambda e, sp_=sp_, ci=ci, tb2=tb2, s5=s5: e.matmul(
                                ps_ld[0:16, tb2 * 512:(tb2 + 1) * 512], ldA_sb[sp_][:, ci, :],
                                x2s[:, ci, tb2 * 512:(tb2 + 1) * 512],
                                start=(s5 == 0 and ci == 0), stop=(ci == SC - 1)), inc)
                            if inc:
                                ld_mm_pe = r
                    ldA_pe_free[sp_] = ld_mm_pe
                    if S + 1 < NG * NS:
                        npar = (S + 1) % 2
                        dq_next = []
                        for ci in range(SC):
                            c2 = ((S + 1) % NS) * SC + ci
                            wt("g", "pe", wdn_free[npar])
                            dq_next.append(op("g", lambda e, c2=c2, ci=ci, npar=npar: e.dma_start(
                                wdn[npar][ci][:], wdnq_d[c2]), "wl", 16))
                    if s5 == 0:
                        acc_init = [None] * 8
                        for tcg in range(8):
                            if gi > 0:
                                wtt("g", out_dma[gi - 1][tcg])
                            acc_init[tcg] = op("g", lambda e, tcg=tcg, gcol=gcol: e.dma_start(
                                ACC[:, tcg, :], attn_scr[gcol + tcg * 128:gcol + (tcg + 1) * 128, :]), "wl", 16)
                    if s5 == NS - 1 and gi + 1 < NG:
                        xng_toks_next = [None] * 8
                        for tc8 in range(8):
                            tc = (gi + 1) * 8 + tc8
                            wt("y", "pe", la_pe_free)
                            xng_toks_next[tc8] = op("y", lambda e, tc=tc, tc8=tc8: e.dma_start(
                                xng[:, :, tc8 * 128:(tc8 + 1) * 128], xnT_scr[tc]), "hw", 16)
                        for cj in range(2):
                            slotj = s_use % 2
                            s_use += 1
                            wt("g", "dv", s_slot_user[slotj])
                            pre_us[cj] = (op("g", lambda e, cj=cj, slotj=slotj: e.dma_start(
                                s_stage[slotj][:].rearrange("p (k c) -> p k c", k=KD),
                                wups_d[cj]), "wl", 16), slotj)
                    relu_all = cnt["ac"]
                    dn_last = 0
                    for hf in range(2):
                        for tcg in range(8):
                            dp = (hf * 8 + tcg) % 2
                            wt("t", "ac", relu_all)
                            bk, bv = pdown_free[dp]
                            wt("t", bk, bv)
                            for ci in range(SC):
                                wt("t", "dv", deq_dn_arr[ci])
                                for nb2 in range(2):
                                    inc = "pe" if (ci == SC - 1 and nb2 == 1) else None
                                    r = op("t", lambda e, dp=dp, ci=ci, nb2=nb2, tcg=tcg, hf=hf, par=par: e.matmul(
                                        pdown[dp][:, nb2 * 512:(nb2 + 1) * 512],
                                        x2s[:, ci, tcg * 128:(tcg + 1) * 128],
                                        wdn[par][ci][:, hf * 1024 + nb2 * 512:hf * 1024 + (nb2 + 1) * 512],
                                        start=(ci == 0), stop=(ci == SC - 1 and nb2 == 1)), inc)
                                    if inc:
                                        dn_mm = r
                            dn_last = dn_mm
                            wt("v", "pe", dn_mm)
                            wtt("v", acc_init[tcg])
                            aad = op("v", lambda e, tcg=tcg, hf=hf, dp=dp: e.tensor_tensor(
                                ACC[:, tcg, hf * 1024:(hf + 1) * 1024],
                                ACC[:, tcg, hf * 1024:(hf + 1) * 1024],
                                pdown[dp][:], ALU.add), "dv")
                            pdown_free[dp] = ("dv", aad)
                    pe_x2_free = dn_last
                    wdn_free[par] = dn_last
                wt("v", "pe", ld_mm_pe)
                wt("v", "pe", la_pe_free)
                if gi > 0:
                    wt("v", "pe", ldb_pe_free)
                ldb_ev = op("v", lambda e: e.tensor_copy(ldb[:], ps_ld[0:16, :]), "dv")
                psld_free = ldb_ev
                if dbg and gi == 0:
                    wt("g", "ac", cnt["ac"])
                    op("g", lambda e: e.dma_start(dbg_x2, x2s[:]), "wl", 16)
                bd_adds = [[0, 0] for _ in range(8)]
                bd_prev_mm = ldb_pe_free
                for jh in range(2):
                    wt("g", "pe", bd_prev_mm)
                    bdld = op("g", lambda e, jh=jh: e.dma_start(
                        Bd_sb[:], ldB_d[:, jh * 1024:(jh + 1) * 1024]), "wl", 16)
                    for tcg in range(8):
                        dp = (jh * 8 + tcg) % 2
                        wt("t", "dv", ldb_ev)
                        wtt("t", bdld)
                        bk, bv = pdown_free[dp]
                        wt("t", bk, bv)
                        op("t", lambda e, dp=dp, tcg=tcg: e.matmul(
                            pdown[dp][:, 0:512], ldb[:, tcg * 128:(tcg + 1) * 128],
                            Bd_sb[:, 0:512], start=True, stop=False))
                        bd_mm = op("t", lambda e, dp=dp, tcg=tcg: e.matmul(
                            pdown[dp][:, 512:1024], ldb[:, tcg * 128:(tcg + 1) * 128],
                            Bd_sb[:, 512:1024], start=True, stop=True), "pe")
                        ldb_pe_free = bd_mm
                        wt("v", "pe", bd_mm)
                        bd_adds[tcg][jh] = op("v", lambda e, tcg=tcg, jh=jh, dp=dp: e.tensor_tensor(
                            ACC[:, tcg, jh * 1024:(jh + 1) * 1024],
                            ACC[:, tcg, jh * 1024:(jh + 1) * 1024],
                            pdown[dp][:], ALU.add), "dv")
                        pdown_free[dp] = ("dv", bd_adds[tcg][jh])
                    bd_prev_mm = ldb_pe_free
                for tcg in range(8):
                    wt("g", "dv", max(bd_adds[tcg]))
                    xa = op("g", lambda e, tcg=tcg, gcol=gcol: e.dma_start(
                        ACC[:, tcg, :], x_d[gcol + tcg * 128:gcol + (tcg + 1) * 128, :],
                        accum_op=ALU.add), "wl", 16)
                    wtt("y", xa)
                    out_dma[gi][tcg] = op("y", lambda e, tcg=tcg, gcol=gcol: e.dma_start(
                        out_d[gcol + tcg * 128:gcol + (tcg + 1) * 128, :],
                        ACC[:, tcg, :]), "out", 16)
            if dbg:
                wts("g", fence("wl"))
            wts("y", fence("out"))
            B.run(nc)

    return nc


def _prep_inputs(L0_attn_Wqkv, L0_attn_Wg, L0_attn_Wout, L0_attn_norm_w,
                 L0_ffn_w_up_q, L0_ffn_w_up_scale, L0_ffn_b_up,
                 L0_ffn_w_up_lora_a, L0_ffn_w_up_lora_b,
                 L0_ffn_w_down_q, L0_ffn_w_down_scale, L0_ffn_b_down,
                 L0_ffn_w_down_lora_a, L0_ffn_w_down_lora_b):
    C = np.ascontiguousarray
    f = lambda a: np.asarray(a, np.float32)
    wqkv_p = C(f(L0_attn_Wqkv).reshape(D, H, 384).transpose(1, 0, 2)
               .reshape(H, KD, 128, 384).transpose(0, 2, 1, 3))
    wg_p = C(f(L0_attn_Wg).reshape(KD, 128, H, 128).transpose(2, 1, 0, 3))
    wout_p = C(f(L0_attn_Wout).reshape(KD, 128, 4, 512).transpose(2, 1, 0, 3))
    nw_p = C(f(L0_attn_norm_w).reshape(H, 128).T)
    wupq_p = C(np.asarray(L0_ffn_w_up_q, np.int32).T
               .reshape(KD, 128, KH, 128).transpose(2, 1, 0, 3))
    wups_p = C(np.repeat(f(L0_ffn_w_up_scale), 64, axis=1).T
               .reshape(KD, 128, KH, 128).transpose(2, 1, 0, 3))
    bup_p = C(f(L0_ffn_b_up).reshape(KH, 128).T)
    laA_p = C(f(L0_ffn_w_up_lora_a).reshape(KD, 128, R).transpose(1, 0, 2))
    laB_p = C(f(L0_ffn_w_up_lora_b))
    wdnq_p = C(np.asarray(L0_ffn_w_down_q, np.int32).T.reshape(KH, 128, D))
    wdns_p = C(np.repeat(f(L0_ffn_w_down_scale), 64, axis=1).T.reshape(KH, 128, D))
    bdn_p = C(f(L0_ffn_b_down).reshape(1, D))
    ldA_p = C(f(L0_ffn_w_down_lora_a).reshape(KH, 128, R).transpose(1, 0, 2))
    ldB_p = C(f(L0_ffn_w_down_lora_b))
    return dict(wqkv=wqkv_p, wg=wg_p, wout=wout_p, nw=nw_p, wupq=wupq_p,
                wups=wups_p, bup=bup_p, laA=laA_p, laB=laB_p, wdnq=wdnq_p,
                wdns=wdns_p, bdn=bdn_p, ldA=ldA_p, ldB=ldB_p)


_NC_CACHE = {}


def kernel(L0_attn_x, L0_attn_Wqkv, L0_attn_Wg, L0_attn_Wout, L0_attn_norm_w,
           L0_ffn_w_up_q, L0_ffn_w_up_scale, L0_ffn_b_up, L0_ffn_w_up_lora_a,
           L0_ffn_w_up_lora_b, L0_ffn_w_down_q, L0_ffn_w_down_scale,
           L0_ffn_b_down, L0_ffn_w_down_lora_a, L0_ffn_w_down_lora_b):
    if "nc" not in _NC_CACHE:
        _NC_CACHE["nc"] = build_nc(dbg=False)
    nc = _NC_CACHE["nc"]
    shared = _prep_inputs(L0_attn_Wqkv, L0_attn_Wg, L0_attn_Wout, L0_attn_norm_w,
                          L0_ffn_w_up_q, L0_ffn_w_up_scale, L0_ffn_b_up,
                          L0_ffn_w_up_lora_a, L0_ffn_w_up_lora_b,
                          L0_ffn_w_down_q, L0_ffn_w_down_scale, L0_ffn_b_down,
                          L0_ffn_w_down_lora_a, L0_ffn_w_down_lora_b)
    x = np.asarray(L0_attn_x, np.float32)
    in_maps = []
    for i in range(8):
        m = dict(shared)
        m["x"] = np.ascontiguousarray(x[i])
        in_maps.append(m)
    res = run_bass_kernel_spmd(nc, in_maps, core_ids=list(range(8)))
    return np.stack([r["out"] for r in res.results], axis=0)


# version 25
# speedup vs baseline: 1.0139x; 1.0139x over previous
import math
import numpy as np
import ml_dtypes
from contextlib import ExitStack

import concourse.bass as bass
import concourse.mybir as mybir
from concourse.bass_utils import run_bass_kernel_spmd

F32 = mybir.dt.float32
BF16 = mybir.dt.bfloat16
I32 = mybir.dt.int32
AF = mybir.ActivationFunctionType
ALU = mybir.AluOpType
AX = mybir.AxisListType

H, DH, BLK, NT, D, HH, R = 16, 128, 256, 2048, 2048, 8192, 16
NB = NT // BLK
KD = D // 128
KH = HH // 128
NTC = NT // 128
NG = 2
TG = NT // NG
NS = 8
SC = 8
EPS_LN, EPS_RMS = 1e-5, 1e-6

BF = ml_dtypes.bfloat16


def _slopes():
    start = 2.0 ** (-(2.0 ** -(math.log2(H) - 3)))
    return np.array([start * start ** i for i in range(H)], dtype=np.float64)


def _consts():
    s = _slopes()
    n = np.arange(128)[:, None]
    m = np.arange(256)[None, :]
    maskc = np.zeros((128, H, 384), dtype=np.float32)
    qdbc = np.zeros((128, H, 256), dtype=np.float32)
    kdec = np.zeros((128, H, 2), dtype=np.float32)
    for h in range(H):
        mm_ = np.exp(-s[h] * (m - n)) * (m >= n)
        maskc[:, h, 0:256] = mm_.astype(np.float32)
        maskc[:, h, 256:384] = mm_[:, 0:128].astype(np.float32)
        qdbc[:, h, :] = np.exp(-s[h] * (np.arange(256) + 1.0))[None, :].astype(np.float32)
        for c in range(2):
            kdec[:, h, c] = np.exp(-s[h] * (255.0 - (c * 128 + np.arange(128)))).astype(np.float32)
    bdec = [float(np.exp(-s[h] * BLK)) for h in range(H)]
    return maskc, qdbc.astype(BF), kdec, bdec


class _ES:

    def __init__(self):
        self.q = {k: [] for k in "tvsgy"}

    def add(self, ek, fn):
        self.q[ek].append(fn)

    def run(self, nc):
        def mk(lst):
            def body(e):
                for f in lst:
                    f(e)
            return body
        with nc.Block() as blk:
            if self.q["g"]:
                blk.gpsimd(mk(self.q["g"]))
            if self.q["y"]:
                blk.sync(mk(self.q["y"]))
            if self.q["t"]:
                blk.tensor(mk(self.q["t"]))
            if self.q["v"]:
                blk.vector(mk(self.q["v"]))
            if self.q["s"]:
                blk.scalar(mk(self.q["s"]))
        self.q = {k: [] for k in "tvsgy"}


def build_nc(dbg=False):
    nc = bass.Bass("TRN2", target_bir_lowering=False)

    x_d = nc.dram_tensor("x", [NT, D], F32, kind="ExternalInput").ap()
    wqkv_d = nc.dram_tensor("wqkv", [H, 128, KD, 384], F32, kind="ExternalInput").ap()
    wg_d = nc.dram_tensor("wg", [H, 128, KD, 128], F32, kind="ExternalInput").ap()
    wout_d = nc.dram_tensor("wout", [4, 128, KD, 512], F32, kind="ExternalInput").ap()
    nw_d = nc.dram_tensor("nw", [128, H], F32, kind="ExternalInput").ap()
    wupq_d = nc.dram_tensor("wupq", [KH, 128, KD, 128], I32, kind="ExternalInput").ap()
    wups_d = nc.dram_tensor("wups", [KH, 128, KD, 128], F32, kind="ExternalInput").ap()
    bup_d = nc.dram_tensor("bup", [128, KH], F32, kind="ExternalInput").ap()
    laA_d = nc.dram_tensor("laA", [128, KD, R], F32, kind="ExternalInput").ap()
    laB_d = nc.dram_tensor("laB", [R, HH], F32, kind="ExternalInput").ap()
    wdnq_d = nc.dram_tensor("wdnq", [KH, 128, D], I32, kind="ExternalInput").ap()
    wdns_d = nc.dram_tensor("wdns", [KH, 128, D], F32, kind="ExternalInput").ap()
    bdn_d = nc.dram_tensor("bdn", [1, D], F32, kind="ExternalInput").ap()
    ldA_d = nc.dram_tensor("ldA", [128, KH, R], F32, kind="ExternalInput").ap()
    ldB_d = nc.dram_tensor("ldB", [R, D], F32, kind="ExternalInput").ap()
    out_d = nc.dram_tensor("out", [NT, D], F32, kind="ExternalOutput").ap()

    attn_scr = nc.dram_tensor("attn_scr", [NT, D], BF16, kind="Internal").ap()
    xnT_scr = nc.dram_tensor("xnT_scr", [NTC, 128, KD, 128], BF16, kind="Internal").ap()
    scr_r = nc.dram_tensor("scr_r", [16, 128], F32, kind="Internal").ap()

    maskc_np, qdbc_np, kdec_np, bdec = _consts()
    maskc_d = nc.inline_tensor(maskc_np.astype(BF), "c_maskc")
    qdbc_d = nc.inline_tensor(qdbc_np, "c_qdbc")
    kdec_d = nc.inline_tensor(kdec_np, "c_kdec")
    identb_d = nc.inline_tensor(np.eye(128, dtype=BF), "c_identb")
    identf_d = nc.inline_tensor(np.eye(128, dtype=np.float32), "c_identf")

    if dbg:
        dbg_xnT = nc.dram_tensor("dbg_xnT", [128, KD, NT], F32, kind="ExternalOutput").ap()
        dbg_OT = nc.dram_tensor("dbg_OT", [128, H, NT], F32, kind="ExternalOutput").ap()
        dbg_og = nc.dram_tensor("dbg_og", [128, H, NT], F32, kind="ExternalOutput").ap()
        dbg_attn = nc.dram_tensor("dbg_attn", [NT, D], F32, kind="ExternalOutput").ap()
        dbg_r = nc.dram_tensor("dbg_r", [128, 48], F32, kind="ExternalOutput").ap()
        dbg_x2 = nc.dram_tensor("dbg_x2", [128, SC, TG], F32, kind="ExternalOutput").ap()

    cnt = {"pe": 0, "dv": 0, "ac": 0}
    NLANES = {"wl": 8, "hw": 4, "out": 4}

    with ExitStack() as S0:
        E0 = S0.enter_context
        sem = {
            "pe": E0(nc.semaphore("s_pe")), "dv": E0(nc.semaphore("s_dv")),
            "ac": E0(nc.semaphore("s_ac")),
        }
        for _cls, _nl in NLANES.items():
            for _i in range(_nl):
                _k = f"{_cls}{_i}"
                sem[_k] = E0(nc.semaphore("s_" + _k))
                cnt[_k] = 0
        dma_idx = {"wl": 0, "hw": 0, "out": 0}
        B = _ES()

        def wt(ek, sk, val):
            if val:
                B.add(ek, lambda e, s=sem[sk], v=val: e.wait_ge(s, v))

        def wtt(ek, tok):
            if tok:
                wt(ek, tok[0], tok[1])

        def wts(ek, toks):
            for t in toks:
                wtt(ek, t)

        def fence(cls):
            return [(f"{cls}{i}", cnt[f"{cls}{i}"]) for i in range(NLANES[cls])
                    if cnt[f"{cls}{i}"]]

        def op(ek, fn, inc=None, n=1):
            if inc is None:
                B.add(ek, fn)
                return None
            if inc in NLANES:
                lane = dma_idx[inc] % NLANES[inc]
                dma_idx[inc] += 1
                key = f"{inc}{lane}"
                cnt[key] += 16
                tok = (key, cnt[key])
                B.add(ek, lambda e, fn=fn, s=sem[key]: fn(e).then_inc(s, 16))
                return tok
            cnt[inc] += n
            tgt = cnt[inc]
            B.add(ek, lambda e, fn=fn, s=sem[inc], n=n: fn(e).then_inc(s, n))
            return tgt

        epsb = E0(nc.sbuf_tensor("epsb", [128, 2], F32))
        identb = E0(nc.sbuf_tensor("identb", [128, 128], BF16))
        ones_col = E0(nc.sbuf_tensor("ones_col", [128, 1], BF16))
        ones1b = E0(nc.sbuf_tensor("ones1b", [1, 128], BF16))

        with ExitStack() as S1:
            E1 = S1.enter_context
            xnT = E1(nc.sbuf_tensor("xnT", [128, KD, NT], BF16))
            OT = E1(nc.sbuf_tensor("OT", [128, H, NT], BF16))
            ps_rms = E1(nc.psum_tensor("ps_rms", [128, 16], F32))
            pb = [E1(nc.psum_tensor(f"pb{i}", [128, 512], F32)) for i in range(4)]
            TR = [E1(nc.psum_tensor(f"ptr{i}", [128, 512], BF16)) for i in range(2)]
            psK = E1(nc.psum_tensor("psK", [128, 256], F32))

            with ExitStack() as SP:
                E = SP.enter_context
                xt = [E(nc.sbuf_tensor(f"xt{i}", [128, D], F32)) for i in range(4)]
                xsq = [E(nc.sbuf_tensor(f"xsq{i}", [128, D], BF16)) for i in range(4)]
                xn_tok = [E(nc.sbuf_tensor(f"xntok{i}", [128, D], BF16)) for i in range(4)]
                st = E(nc.sbuf_tensor("lnst", [128, 4, 8], F32))

                idb_tok = op("y", lambda e: e.dma_start(identb[:], identb_d.ap()), "hw", 16)
                op("v", lambda e: e.memset(ones_col[:], 1.0), "dv")
                op("v", lambda e: e.memset(ones1b[:], 1.0), "dv")
                op("v", lambda e: e.memset(epsb[:, 0:1], EPS_LN), "dv")
                epsv = op("v", lambda e: e.memset(epsb[:, 1:2], EPS_RMS), "dv")

                apply_inc = [0] * NTC
                xnw_toks = [None] * NTC
                sqd_inc = [0] * NTC
                e2_arr = [0] * NTC
                trev_inc = [0] * (NTC * 4)
                trmm_inc = [0] * (NTC * 4)
                xld = [None] * NTC
                for tp_ in range(NTC // 4):
                    tcs = tuple(4 * tp_ + j_ for j_ in range(4))
                    e1 = [0] * 4
                    e2 = [0] * 4
                    e3 = [0] * 4
                    e4 = [0] * 4
                    e5 = [0] * 4
                    e6 = [0] * 4
                    e7 = [0] * 4
                    for i, tc in enumerate(tcs):
                        ii = tc % 4
                        if tc >= 4:
                            wt("y", "dv", apply_inc[tc - 4])
                            wt("y", "ac", sqd_inc[tc - 4])
                        xld[tc] = op("y", lambda e, tc=tc, ii=ii: e.dma_start(
                            xt[ii][:], x_d[tc * 128:(tc + 1) * 128, :]), "hw", 16)
                    for i, tc in enumerate(tcs):
                        ii = tc % 4
                        wtt("v", xld[tc])
                        e1[i] = op("v", lambda e, i=i, ii=ii: e.reduce_sum(st[:, i, 0:1], xt[ii][:], AX.X), "dv")
                    for i, tc in enumerate(tcs):
                        ii = tc % 4
                        wtt("s", xld[tc])
                        if tc >= 4:
                            wt("s", "dv", e2_arr[tc - 4])
                        sqd_inc[tc] = op("s", lambda e, i=i, ii=ii: e.activation(
                            xsq[i][:], xt[ii][:], AF.Square), "ac")
                    for i, tc in enumerate(tcs):
                        wt("v", "ac", sqd_inc[tc])
                        e2[i] = op("v", lambda e, i=i: e.reduce_sum(st[:, i, 1:2], xsq[i][:], AX.X), "dv")
                        e2_arr[tc] = e2[i]
                    for i, tc in enumerate(tcs):
                        wt("v", "dv", e2[i])
                        e3[i] = op("v", lambda e, i=i: e.tensor_scalar(
                            st[:, i, 2:4], st[:, i, 0:2], 1.0 / D, None, ALU.mult), "dv")
                    for i, tc in enumerate(tcs):
                        wt("v", "dv", e3[i])
                        e4[i] = op("v", lambda e, i=i: e.tensor_tensor(
                            st[:, i, 4:5], st[:, i, 2:3], st[:, i, 2:3], ALU.mult), "dv")
                    for i, tc in enumerate(tcs):
                        wt("v", "dv", e4[i])
                        e5[i] = op("v", lambda e, i=i: e.tensor_tensor(
                            st[:, i, 5:6], st[:, i, 3:4], st[:, i, 4:5], ALU.subtract), "dv")
                    for i, tc in enumerate(tcs):
                        wt("s", "dv", e5[i])
                        if tp_ == 0 and i == 0:
                            wt("s", "dv", epsv)
                        e6[i] = op("s", lambda e, i=i: e.activation(
                            st[:, i, 6:7], st[:, i, 5:6], AF.Sqrt, bias=epsb[:, 0:1]), "ac")
                    for i, tc in enumerate(tcs):
                        wt("v", "ac", e6[i])
                        e7[i] = op("v", lambda e, i=i: e.reciprocal(st[:, i, 7:8], st[:, i, 6:7]), "dv")
                    for i, tc in enumerate(tcs):
                        wt("v", "dv", e7[i])
                        if tc >= 4:
                            wt("v", "pe", trmm_inc[(tc - 4) * 4 + 3])
                        apply_inc[tc] = op("v", lambda e, i=i, ii=tc % 4: e.tensor_scalar(
                            xn_tok[i][:], xt[ii][:], st[:, i, 2:3], st[:, i, 7:8],
                            ALU.subtract, ALU.mult), "dv")
                    for i, tc in enumerate(tcs):
                        for grp in range(4):
                            gidx = tc * 4 + grp
                            tp2 = gidx % 2
                            wt("t", "dv", apply_inc[tc])
                            if gidx >= 2:
                                wt("t", "dv", trev_inc[gidx - 2])
                            if gidx == 0:
                                wtt("t", idb_tok)
                            for j in range(4):
                                k = grp * 4 + j
                                inc = "pe" if j == 3 else None
                                r = op("t", lambda e, tp2=tp2, j=j, k=k, i=i: e.transpose(
                                    TR[tp2][:, j * 128:(j + 1) * 128],
                                    xn_tok[i][:, k * 128:(k + 1) * 128], identb[:]), inc)
                                if j == 3:
                                    trmm_inc[gidx] = r
                            wt("v", "pe", trmm_inc[gidx])
                            trev_inc[gidx] = op("v", lambda e, tp2=tp2, tc=tc, grp=grp: e.tensor_copy(
                                xnT[:, grp * 4:(grp + 1) * 4, tc * 128:(tc + 1) * 128],
                                TR[tp2][:].rearrange("p (j c) -> p j c", j=4)), "dv")
                        wt("g", "dv", trev_inc[tc * 4 + 3])
                        xnw_toks[tc] = op("g", lambda e, tc=tc: e.dma_start(
                            xnT_scr[tc], xnT[:, :, tc * 128:(tc + 1) * 128]), "wl", 16)
                B.run(nc)

            if dbg:
                with ExitStack() as SP:
                    op("g", lambda e: e.dma_start(dbg_xnT, xnT[:]), "wl", 16)
                    wts("g", fence("wl"))
                    B.run(nc)

            with ExitStack() as SP:
                E = SP.enter_context
                whb = [E(nc.sbuf_tensor(f"whb{i}", [128, KD, 384], BF16)) for i in range(2)]
                qkv = [E(nc.sbuf_tensor(f"qkv{i}", [128, 3, NT], BF16)) for i in range(2)]
                maskp = [E(nc.sbuf_tensor(f"maskp{i}", [128, 2, 384], BF16)) for i in range(2)]
                qdbp = [E(nc.sbuf_tensor(f"qdbp{i}", [128, 2, 256], BF16)) for i in range(2)]
                kdec_sb = E(nc.sbuf_tensor("kdec", [128, H, 2], F32))
                Sd = [E(nc.sbuf_tensor(f"Sd{i}", [128, 384], BF16)) for i in range(2)]
                vtok = [E(nc.sbuf_tensor(f"vtok{i}", [128, 2, 2, 128], BF16)) for i in range(2)]
                kdt = [E(nc.sbuf_tensor(f"kdt{i}", [128, 2, 2, 128], BF16)) for i in range(2)]
                qd = [E(nc.sbuf_tensor(f"qd{i}", [128, 256], BF16)) for i in range(2)]
                sqb = E(nc.sbuf_tensor("sqb", [128, 2, NB, 256], BF16))
                kv2 = [E(nc.sbuf_tensor(f"kv{i}", [128, 128], F32)) for i in range(2)]
                kvbf2 = [E(nc.sbuf_tensor(f"kvbf{i}", [128, 128], BF16)) for i in range(2)]
                wgh = [E(nc.sbuf_tensor("wgh0", [128, KD, 2, 128], BF16))] * 2
                g_sb = [E(nc.sbuf_tensor(f"gsb{i}", [128, 512], BF16)) for i in range(2)]
                ogt = g_sb
                nw_sb = E(nc.sbuf_tensor("nwsb", [128, H], F32))

                kdl = op("y", lambda e: e.dma_start(kdec_sb[:], kdec_d.ap()), "hw", 16)
                nwl = op("y", lambda e: e.dma_start(nw_sb[:], nw_d), "hw", 16)

                whb_pe_free = [0, 0]
                qkv_pe_last = [0, 0]
                qkv_dv_last = [0, 0]
                silu_done = [0, 0]
                bank_free = {i: ("ac", 0) for i in range(4)}
                trbank_free = [None, None]
                sd_pe_free = [0, 0]
                vk_pe_free = [[0, 0], [0, 0]]
                psK_free = [0, 0]
                pbO_free = [("dv", 0), ("dv", 0)]
                pbS_free = [("dv", 0), ("dv", 0)]
                kvbf_src = [("dv", 0), ("dv", 0)]
                last_kvc = [0, 0]
                mask_dv_free = [0, 0]
                pending_rms = None
                rms_pe_last = 0
                first_rms = [True]
                wgh_pe_free = [0, 0]
                ogpair_dv_free = [0, 0]
                og_dv_last = 0

                def emit_rms(sq_last):
                    wt("t", "dv", sq_last)
                    r = 0
                    for p_ in range(2):
                        for b in range(NB):
                            for mc in range(2):
                                fr = first_rms[0]
                                first_rms[0] = False
                                inc = "pe" if (p_ == 1 and b == NB - 1 and mc == 1) else None
                                r = op("t", lambda e, p_=p_, b=b, mc=mc, fr=fr: e.matmul(
                                    ps_rms[:, 2 * b + mc:2 * b + mc + 1],
                                    sqb[:, p_, b, mc * 128:(mc + 1) * 128], ones_col[:],
                                    start=fr, stop=True), inc) or r
                    return r

                for hp in range(H // 2):
                    pp = hp % 2
                    heads = (2 * hp, 2 * hp + 1)
                    wt("y", "dv", mask_dv_free[pp])
                    mtok = op("y", lambda e, hp=hp, pp=pp: e.dma_start(
                        maskp[pp][:], maskc_d.ap()[:, 2 * hp:2 * hp + 2, :]), "hw", 16)
                    qtok = op("y", lambda e, hp=hp, pp=pp: e.dma_start(
                        qdbp[pp][:], qdbc_d.ap()[:, 2 * hp:2 * hp + 2, :]), "hw", 16)
                    for p_ in range(2):
                        h = heads[p_]
                        wt("g", "pe", whb_pe_free[p_])
                        wq = op("g", lambda e, h=h, p_=p_: e.dma_start(
                            whb[p_][:], wqkv_d[h]), "wl", 16)
                        for f in range(3):
                            for tb in range(4):
                                rot = (f * 4 + tb) % 4
                                wtt("t", wq)
                                bk, bv = bank_free[rot]
                                wt("t", bk, bv)
                                if rot < 2:
                                    bk2, bv2 = pbS_free[rot]
                                    wt("t", bk2, bv2)
                                else:
                                    bk2, bv2 = pbO_free[rot - 2]
                                    wt("t", bk2, bv2)
                                mmv = None
                                for k in range(KD):
                                    inc = "pe" if k == KD - 1 else None
                                    mmv = op("t", lambda e, p_=p_, k=k, f=f, tb=tb, rot=rot: e.matmul(
                                        pb[rot][:], whb[p_][:, k, f * 128:(f + 1) * 128],
                                        xnT[:, k, tb * 512:(tb + 1) * 512],
                                        start=(k == 0), stop=(k == KD - 1)), inc) or mmv
                                whb_pe_free[p_] = mmv
                                wt("s", "pe", mmv)
                                if hp > 0:
                                    wt("s", "pe", qkv_pe_last[p_])
                                    wt("s", "dv", qkv_dv_last[p_])
                                sv = op("s", lambda e, p_=p_, f=f, tb=tb, rot=rot: e.activation(
                                    qkv[p_][:, f, tb * 512:(tb + 1) * 512], pb[rot][:], AF.Silu), "ac")
                                bank_free[rot] = ("ac", sv)
                                silu_done[p_] = sv
                        wt("v", "dv", last_kvc[p_])
                        op("v", lambda e, p_=p_: e.memset(kv2[p_][:], 0.0), "dv")
                        kvi = op("v", lambda e, p_=p_: e.memset(kvbf2[p_][:], 0.0), "dv")
                        kvbf_src[p_] = ("dv", kvi)
                    if pending_rms is not None:
                        rms_pe_last = emit_rms(pending_rms)
                        pending_rms = None
                    oev = [[0] * NB, [0] * NB]
                    tr7 = [0, 0]
                    pending_kv = None

                    def emit_kvchain(p2, h2, blc2, o_mm2, kvd_mm2, b2):
                        wt("v", "pe", o_mm2)
                        oev[p2][b2] = op("v", lambda e, p2=p2, h2=h2, blc2=blc2: e.tensor_copy(
                            OT[:, h2, blc2:blc2 + 256], pb[2 + p2][:, 0:256]), "dv")
                        pbO_free[p2] = ("dv", oev[p2][b2])
                        wt("v", "pe", kvd_mm2)
                        psK_free[p2] = op("v", lambda e, p2=p2, h2=h2: e.scalar_tensor_tensor(
                            kv2[p2][:], kv2[p2][:], bdec[h2], psK[:, p2 * 128:(p2 + 1) * 128],
                            ALU.mult, ALU.add), "dv")
                        if b2 < NB - 1:
                            wt("v", "dv", psK_free[p2])
                            kvc = op("v", lambda e, p2=p2: e.tensor_copy(
                                kvbf2[p2][:], kv2[p2][:]), "dv")
                            kvbf_src[p2] = ("dv", kvc)
                            last_kvc[p2] = kvc
                    for b in range(NB):
                        blc = b * 256
                        for p_ in range(2):
                            h = heads[p_]
                            wt("t", "ac", silu_done[p_])
                            bk, bv = pbS_free[p_]
                            wt("t", bk, bv)
                            op("t", lambda e, p_=p_, blc=blc: e.matmul(
                                pb[p_][:, 0:256], qkv[p_][:, 1, blc:blc + 128],
                                qkv[p_][:, 0, blc:blc + 256], start=True, stop=False))
                            s_mm = op("t", lambda e, p_=p_, blc=blc: e.matmul(
                                pb[p_][:, 256:384], qkv[p_][:, 1, blc + 128:blc + 256],
                                qkv[p_][:, 0, blc + 128:blc + 256], start=False, stop=True), "pe")
                            if trbank_free[p_]:
                                wt("t", trbank_free[p_][0], trbank_free[p_][1])
                            tr_half = 0
                            for j, (fch, off) in enumerate(((2, 0), (2, 128), (1, 0), (1, 128))):
                                inc = "pe" if j in (1, 3) else None
                                r = op("t", lambda e, p_=p_, j=j, fch=fch, off=off, blc=blc: e.transpose(
                                    TR[p_][:, j * 128:(j + 1) * 128],
                                    qkv[p_][:, fch, blc + off:blc + off + 128], identb[:]), inc)
                                if j == 1:
                                    tr_half = r
                                if j == 3:
                                    tr_mm = r
                            tr7[p_] = tr_mm
                            if b == 0:
                                wtt("v", mtok)
                                wtt("v", qtok)
                                if hp == 0 and p_ == 0:
                                    wtt("v", kdl)
                            wt("v", "pe", s_mm)
                            wt("v", "pe", sd_pe_free[p_])
                            sd_done = op("v", lambda e, p_=p_, pp=pp: e.tensor_tensor(
                                Sd[p_][:, 0:384], pb[p_][:, 0:384], maskp[pp][:, p_, :], ALU.mult), "dv")
                            pbS_free[p_] = ("dv", sd_done)
                            qd_done = op("v", lambda e, p_=p_, pp=pp, blc=blc: e.tensor_tensor(
                                qd[p_][:], qkv[p_][:, 0, blc:blc + 256],
                                qdbp[pp][:, p_, :], ALU.mult), "dv")
                            qkv_dv_last[p_] = qd_done
                            mask_dv_free[pp] = qd_done
                            if pending_kv is not None:
                                emit_kvchain(*pending_kv)
                                pending_kv = None
                            sp = b % 2
                            wt("s", "pe", tr_half)
                            wt("s", "pe", vk_pe_free[p_][sp])
                            if b == 0 and hp == 0 and p_ == 0:
                                wtt("s", kdl)
                            op("s", lambda e, p_=p_, sp=sp: e.activation(vtok[p_][:, sp, 0, :], TR[p_][:, 0:128], AF.Copy), "ac")
                            op("s", lambda e, p_=p_, sp=sp: e.activation(vtok[p_][:, sp, 1, :], TR[p_][:, 128:256], AF.Copy), "ac")
                            wt("s", "pe", tr_mm)
                            op("s", lambda e, p_=p_, h=h, sp=sp: e.activation(
                                kdt[p_][:, sp, 0, :], TR[p_][:, 256:384], AF.Copy, scale=kdec_sb[:, h, 0:1]), "ac")
                            vk_done = op("s", lambda e, p_=p_, h=h, sp=sp: e.activation(
                                kdt[p_][:, sp, 1, :], TR[p_][:, 384:512], AF.Copy, scale=kdec_sb[:, h, 1:2]), "ac")
                            trbank_free[p_] = ("ac", vk_done)
                            wt("t", "dv", qd_done)
                            wt("t", "ac", vk_done)
                            op("t", lambda e, p_=p_: e.matmul(
                                pb[2 + p_][:, 0:256], kvbf2[p_][:], qd[p_][:], start=True, stop=False))
                            op("t", lambda e, p_=p_, sp=sp: e.matmul(
                                pb[2 + p_][:, 0:256], vtok[p_][:, sp, 0, :], Sd[p_][:, 0:256],
                                start=False, stop=False))
                            o_mm = op("t", lambda e, p_=p_, sp=sp: e.matmul(
                                pb[2 + p_][:, 128:256], vtok[p_][:, sp, 1, :], Sd[p_][:, 256:384],
                                start=False, stop=True), "pe")
                            sd_pe_free[p_] = o_mm
                            op("t", lambda e, p_=p_, sp=sp: e.matmul(
                                psK[:, p_ * 128:(p_ + 1) * 128],
                                kdt[p_][:, sp, 0, :], vtok[p_][:, sp, 0, :], start=(p_ == 0), stop=False))
                            kvd_mm = op("t", lambda e, p_=p_, sp=sp: e.matmul(
                                psK[:, p_ * 128:(p_ + 1) * 128],
                                kdt[p_][:, sp, 1, :], vtok[p_][:, sp, 1, :], start=False, stop=True), "pe")
                            vk_pe_free[p_][sp] = kvd_mm
                            pending_kv = (p_, h, blc, o_mm, kvd_mm, b)
                    if pending_kv is not None:
                        emit_kvchain(*pending_kv)
                        pending_kv = None
                    qkv_pe_last = [tr7[0], tr7[1]]
                    sq_last = 0
                    for p_ in range(2):
                        for b in range(NB):
                            wt("v", "dv", oev[p_][b])
                            if p_ == 0 and b == 0:
                                wt("v", "pe", rms_pe_last)
                            sq_last = op("v", lambda e, p_=p_, b=b, h=heads[p_]: e.tensor_tensor(
                                sqb[:, p_, b, :], OT[:, h, b * 256:b * 256 + 256],
                                OT[:, h, b * 256:b * 256 + 256], ALU.mult), "dv")
                    pending_rms = sq_last
                    wgl = [None, None]
                    for p_ in range(2):
                        h = heads[p_]
                        wt("g", "pe", max(wgh_pe_free))
                        wgl[p_] = op("g", lambda e, h=h, p_=p_, pp=pp: e.dma_start(
                            wgh[pp][:, :, p_, :], wg_d[h]), "wl", 16)
                    for p_ in range(2):
                        h = heads[p_]
                        for tb in range(4):
                            rot = (p_ * 4 + tb) % 4
                            r2 = (p_ * 4 + tb) % 2
                            wtt("t", wgl[p_])
                            bk, bv = bank_free[rot]
                            wt("t", bk, bv)
                            if rot < 2:
                                bk2, bv2 = pbS_free[rot]
                                wt("t", bk2, bv2)
                            else:
                                bk2, bv2 = pbO_free[rot - 2]
                                wt("t", bk2, bv2)
                            mmv = None
                            for k in range(KD):
                                inc = "pe" if k == KD - 1 else None
                                mmv = op("t", lambda e, pp=pp, p_=p_, k=k, tb=tb, rot=rot: e.matmul(
                                    pb[rot][:], wgh[pp][:, k, p_, :],
                                    xnT[:, k, tb * 512:(tb + 1) * 512],
                                    start=(k == 0), stop=(k == KD - 1)), inc) or mmv
                            wgh_pe_free[pp] = mmv
                            wt("s", "pe", mmv)
                            wt("s", "dv", ogpair_dv_free[r2])
                            g_ev = op("s", lambda e, rot=rot, r2=r2: e.activation(
                                g_sb[r2][:], pb[rot][:], AF.Sigmoid), "ac")
                            bank_free[rot] = ("ac", g_ev)
                            wt("v", "ac", g_ev)
                            if hp == 0 and p_ == 0 and tb == 0:
                                wtt("v", nwl)
                            t1 = op("v", lambda e, r2=r2, h=h: e.tensor_scalar(
                                g_sb[r2][:], g_sb[r2][:], nw_sb[:, h:h + 1], None, ALU.mult), "dv")
                            wt("v", "dv", t1)
                            wt("v", "dv", pending_rms)
                            og_dv_last = op("v", lambda e, r2=r2, h=h, tb=tb: e.tensor_tensor(
                                OT[:, h, tb * 512:(tb + 1) * 512],
                                OT[:, h, tb * 512:(tb + 1) * 512], ogt[r2][:], ALU.mult), "dv")
                            ogpair_dv_free[r2] = og_dv_last
                rms_pe_last = emit_rms(pending_rms)
                B.run(nc)

            if dbg:
                with ExitStack() as SP:
                    op("g", lambda e: e.dma_start(dbg_OT, OT[:]), "wl", 16)
                    wts("g", fence("wl"))
                    B.run(nc)

            with ExitStack() as SP:
                E = SP.enter_context
                wstage = [E(nc.sbuf_tensor(f"wstg{i}", [128, KD, 512], BF16)) for i in range(2)]
                astage = [E(nc.sbuf_tensor(f"astg{i}", [128, 512], BF16)) for i in range(2)]
                rms_sb = E(nc.sbuf_tensor("rms", [128, 48], F32))
                bbc = E(nc.sbuf_tensor("bbc", [128, D], BF16))
                bdn_sb = E(nc.sbuf_tensor("bdnsb", [1, D], BF16))

                bdl = op("g", lambda e: e.dma_start(bdn_sb[:], bdn_d), "wl", 16)

                r1 = op("v", lambda e: e.tensor_scalar(
                    rms_sb[:, 0:16], ps_rms[:], 1.0 / D, None, ALU.mult), "dv")
                wt("s", "dv", r1)
                r2 = op("s", lambda e: e.activation(
                    rms_sb[:, 16:32], rms_sb[:, 0:16], AF.Sqrt, bias=epsb[:, 1:2]), "ac")
                wt("v", "ac", r2)
                r3 = op("v", lambda e: e.reciprocal(rms_sb[:, 32:48], rms_sb[:, 16:32]), "dv")
                bbc_ev = 0
                for j in range(4):
                    wtt("t", bdl)
                    wt("t", "dv", bbc_ev)
                    r8 = op("t", lambda e, j=j: e.matmul(
                        pb[0][:], ones1b[:], bdn_sb[0:1, j * 512:(j + 1) * 512],
                        start=True, stop=True), "pe")
                    wt("v", "pe", r8)
                    bbc_ev = op("v", lambda e, j=j: e.tensor_copy(
                        bbc[:, j * 512:(j + 1) * 512], pb[0][:]), "dv")
                if dbg:
                    wt("g", "dv", r3)
                    op("g", lambda e: e.dma_start(dbg_r, rms_sb[:]), "wl", 16)
                    op("g", lambda e: e.dma_start(dbg_og, OT[:]), "wl", 16)

                wstage_pe_free = [0, 0]
                bank_free = {i: ("dv", bbc_ev if i == 0 else 0) for i in range(4)}
                astage_free = [None, None]
                nit = 0
                for st4 in range(4):
                    p_ = st4 % 2
                    wt("g", "pe", wstage_pe_free[p_])
                    wo_ld = op("g", lambda e, st4=st4, p_=p_: e.dma_start(
                        wstage[p_][:], wout_d[st4]), "wl", 16)
                    for tc in range(NTC):
                        rot = nit % 4
                        r2p = nit % 2
                        wtt("t", wo_ld)
                        bk, bv = bank_free[rot]
                        wt("t", bk, bv)
                        w_mm = None
                        for k in range(KD):
                            inc = "pe" if k == KD - 1 else None
                            w_mm = op("t", lambda e, p_=p_, k=k, tc=tc, rot=rot: e.matmul(
                                pb[rot][:], OT[:, k, tc * 128:(tc + 1) * 128],
                                wstage[p_][:, k, :], start=(k == 0), stop=(k == KD - 1)),
                                inc) or w_mm
                        wstage_pe_free[p_] = w_mm
                        wt("v", "pe", w_mm)
                        wtt("v", astage_free[r2p])
                        if nit == 0:
                            wt("v", "dv", r3)
                            wt("v", "dv", bbc_ev)
                        a_ev = op("v", lambda e, rot=rot, r2p=r2p, tc=tc, st4=st4: e.scalar_tensor_tensor(
                            astage[r2p][:], pb[rot][:], rms_sb[:, 32 + tc:33 + tc],
                            bbc[:, st4 * 512:(st4 + 1) * 512], ALU.mult, ALU.add), "dv")
                        bank_free[rot] = ("dv", a_ev)
                        wt("y", "dv", a_ev)
                        astage_free[r2p] = op("y", lambda e, r2p=r2p, tc=tc, st4=st4: e.dma_start(
                            attn_scr[tc * 128:(tc + 1) * 128, st4 * 512:(st4 + 1) * 512],
                            astage[r2p][:]), "hw", 16)
                        nit += 1
                attn_wr_done = fence("hw")
                B.run(nc)

            if dbg:
                with ExitStack() as SP:
                    wts("g", attn_wr_done)
                    op("g", lambda e: e.dma_start(dbg_attn, attn_scr), "wl", 16)
                    wts("g", fence("wl"))
                    B.run(nc)

        with ExitStack() as SP:
            E = SP.enter_context
            ACC = E(nc.sbuf_tensor("ACC", [128, 8, D], F32))
            x2s = E(nc.sbuf_tensor("x2s", [128, SC, TG], BF16))
            xng = E(nc.sbuf_tensor("xng", [128, KD, TG], BF16))
            wdn = [[E(nc.sbuf_tensor(f"wdn{p}_{i}", [128, D], BF16)) for i in range(SC)]
                   for p in range(2)]
            s_stage = [E(nc.sbuf_tensor(f"sstage{i}", [128, D], BF16)) for i in range(2)]
            upq = E(nc.sbuf_tensor("upq", [128, KD, 128], I32))
            wupbf = [E(nc.sbuf_tensor(f"wupbf{i}", [128, KD, 128], BF16)) for i in range(2)]
            la_sb = E(nc.sbuf_tensor("lasb", [16, TG], BF16))
            laBs = E(nc.sbuf_tensor("laBs", [16, 1024], BF16))
            ldb = la_sb
            laA_sb = E(nc.sbuf_tensor("laAsb", [128, KD, R], BF16))
            ldA_sb = [E(nc.sbuf_tensor(f"ldAsb{i}", [128, SC, R], BF16)) for i in range(2)]
            Bd_sb = E(nc.sbuf_tensor("Bdsb", [16, 1024], BF16))
            bup_sb = E(nc.sbuf_tensor("bupsb", [128, KH], F32))
            ps_up = [E(nc.psum_tensor(f"psup{i}", [128, 512], F32)) for i in range(2)]
            pdown = [E(nc.psum_tensor(f"pdn{i}", [128, 1024], F32)) for i in range(2)]
            ps_ld = E(nc.psum_tensor("psld", [16, 1024], F32))

            wts("g", attn_wr_done)
            wts("y", attn_wr_done)
            wts("v", attn_wr_done)
            cl1 = op("g", lambda e: e.dma_start(laA_sb[:], laA_d), "wl", 16)
            cl4 = op("y", lambda e: e.dma_start(bup_sb[:], bup_d), "hw", 16)

            out_dma = [[None] * 8 for _ in range(NG)]
            pe_x2_free = 0
            wdn_free = [0, 0]
            dq_next = None
            pdown_free = [("dv", 0), ("dv", 0)]
            psup_free = [("ac", 0), ("ac", 0)]
            s_slot_user = [0, 0]
            s_use = 0
            upq_dv_free = [0, 0]
            wupbf_pe_free = [0, 0]
            psld_free = 0
            ldA_pe_free = [0, 0]
            laBs_pe_free = 0
            ld_mm_pe = 0
            ldb_pe_free = 0
            la_pe_free = 0

            xng_toks_next = None
            pre_us = {}
            for cj in range(2):
                slotj = s_use % 2
                s_use += 1
                pre_us[cj] = (op("g", lambda e, cj=cj, slotj=slotj: e.dma_start(
                    s_stage[slotj][:].rearrange("p (k c) -> p k c", k=KD),
                    wups_d[cj]), "wl", 16), slotj)
            dq_next = []
            for ci in range(SC):
                dq_next.append(op("g", lambda e, ci=ci: e.dma_start(
                    wdn[0][ci][:], wdnq_d[ci]), "wl", 16))

            for gi in range(NG):
                gcol = gi * TG
                if gi == 0:
                    xng_toks = [None] * 8
                    for tc8 in range(8):
                        wtt("y", xnw_toks[tc8])
                        xng_toks[tc8] = op("y", lambda e, tc8=tc8: e.dma_start(
                            xng[:, :, tc8 * 128:(tc8 + 1) * 128], xnT_scr[tc8]), "hw", 16)
                else:
                    xng_toks = xng_toks_next
                la_ev = 0
                for tb2 in range(2):
                    wtt("t", cl1)
                    wts("t", [t for t in xng_toks[tb2 * 4:(tb2 + 1) * 4] if t])
                    bk, bv = psup_free[tb2]
                    wt("t", bk, bv)
                    mmv = None
                    for k in range(KD):
                        inc = "pe" if k == KD - 1 else None
                        mmv = op("t", lambda e, k=k, tb2=tb2: e.matmul(
                            ps_up[tb2][0:16, :], laA_sb[:, k, :],
                            xng[:, k, tb2 * 512:(tb2 + 1) * 512],
                            start=(k == 0), stop=(k == KD - 1)), inc) or mmv
                    wt("v", "pe", mmv)
                    if gi > 0:
                        wt("v", "pe", la_pe_free)
                    la_ev = op("v", lambda e, tb2=tb2: e.tensor_copy(
                        la_sb[:, tb2 * 512:(tb2 + 1) * 512], ps_up[tb2][0:16, :]), "dv")
                    psup_free[tb2] = ("dv", la_ev)

                for s5 in range(NS):
                    S = gi * NS + s5
                    par = S % 2
                    dq_lds = dq_next
                    ds_toks = [None] * SC
                    ds_slots = [0] * SC
                    deq_dn_arr = [0] * SC
                    wt("g", "pe", laBs_pe_free)
                    lb_ld = op("g", lambda e, s5=s5: e.dma_start(
                        laBs[:], laB_d[:, s5 * 1024:(s5 + 1) * 1024]), "wl", 16)
                    sp_ = s5 % 2
                    wt("g", "pe", ldA_pe_free[sp_])
                    ldA_ld = op("g", lambda e, s5=s5, sp_=sp_: e.dma_start(
                        ldA_sb[sp_][:], ldA_d[:, s5 * SC:(s5 + 1) * SC, :]), "wl", 16)
                    relu_ev = 0
                    for ci in range(SC):
                        c = s5 * SC + ci
                        wp = ci % 2
                        if s5 == 0 and ci in pre_us:
                            us_tok, slot = pre_us.pop(ci)
                        else:
                            slot = s_use % 2
                            s_use += 1
                            wt("g", "dv", s_slot_user[slot])
                            us_tok = op("g", lambda e, c=c, slot=slot: e.dma_start(
                                s_stage[slot][:].rearrange("p (k c) -> p k c", k=KD),
                                wups_d[c]), "wl", 16)
                        KH2 = KD // 2
                        dqh = [0, 0]
                        for hk in range(2):
                            wt("y", "dv", upq_dv_free[hk])
                            uq_ld = op("y", lambda e, c=c, hk=hk, KH2=KH2: e.dma_start(
                                upq[:, hk * KH2:(hk + 1) * KH2, :],
                                wupq_d[c][:, hk * KH2:(hk + 1) * KH2, :]), "hw", 16)
                            wtt("v", uq_ld)
                            if hk == 0:
                                wtt("v", us_tok)
                                wt("v", "pe", wupbf_pe_free[wp])
                            dqh[hk] = op("v", lambda e, wp=wp, slot=slot, hk=hk, KH2=KH2: e.scalar_tensor_tensor(
                                wupbf[wp][:, hk * KH2:(hk + 1) * KH2, :],
                                upq[:, hk * KH2:(hk + 1) * KH2, :], -7.5,
                                s_stage[slot][:].rearrange("p (k c) -> p k c", k=KD)[:, hk * KH2:(hk + 1) * KH2, :],
                                ALU.add, ALU.mult), "dv")
                            upq_dv_free[hk] = dqh[hk]
                        dq = dqh[1]
                        s_slot_user[slot] = dq
                        djs = [ci - 1] if ci >= 1 else []
                        if ci == SC - 1:
                            djs += [SC - 1]
                        for dj in djs:
                            slotd = s_use % 2
                            s_use += 1
                            wt("g", "dv", s_slot_user[slotd])
                            ds_toks[dj] = op("g", lambda e, dj=dj, slotd=slotd, s5=s5: e.dma_start(
                                s_stage[slotd][:], wdns_d[s5 * SC + dj]), "wl", 16)
                            ds_slots[dj] = slotd
                            wtt("v", dq_lds[dj])
                            wtt("v", ds_toks[dj])
                            deq_dn_arr[dj] = op("v", lambda e, dj=dj, slotd=slotd, par=par: e.scalar_tensor_tensor(
                                wdn[par][dj][:], wdn[par][dj][:], -7.5, s_stage[slotd][:],
                                ALU.add, ALU.mult), "dv")
                            s_slot_user[slotd] = deq_dn_arr[dj]
                        for tb2 in range(2):
                            if tb2 == 0:
                                wt("t", "dv", dqh[0])
                            wtt("t", lb_ld)
                            wt("t", "dv", la_ev)
                            bk, bv = psup_free[tb2]
                            wt("t", bk, bv)
                            if pe_x2_free:
                                wt("s", "pe", pe_x2_free)
                            for k in range(KD):
                                if tb2 == 0 and k == KD // 2:
                                    wt("t", "dv", dqh[1])
                                op("t", lambda e, wp=wp, k=k, tb2=tb2: e.matmul(
                                    ps_up[tb2][:], wupbf[wp][:, k, :],
                                    xng[:, k, tb2 * 512:(tb2 + 1) * 512],
                                    start=(k == 0), stop=False))
                            upmm = op("t", lambda e, ci=ci, tb2=tb2: e.matmul(
                                ps_up[tb2][:], laBs[:, ci * 128:(ci + 1) * 128],
                                la_sb[:, tb2 * 512:(tb2 + 1) * 512],
                                start=False, stop=True), "pe")
                            wupbf_pe_free[wp] = upmm
                            laBs_pe_free = upmm
                            la_pe_free = upmm
                            wt("s", "pe", upmm)
                            if gi == 0 and s5 == 0 and ci == 0 and tb2 == 0:
                                wtt("s", cl4)
                            relu_ev = op("s", lambda e, ci=ci, tb2=tb2, c=c: e.activation(
                                x2s[:, ci, tb2 * 512:(tb2 + 1) * 512], ps_up[tb2][:],
                                AF.Relu, bias=bup_sb[:, c:c + 1]), "ac")
                            psup_free[tb2] = ("ac", relu_ev)
                    wt("t", "ac", relu_ev)
                    if s5 == 0:
                        wt("t", "dv", psld_free)
                        wtt("t", ldA_ld)
                    else:
                        wtt("t", ldA_ld)
                    for tb2 in range(2):
                        for ci in range(SC):
                            inc = "pe" if (ci == SC - 1 and tb2 == 1) else None
                            r = op("t", lambda e, sp_=sp_, ci=ci, tb2=tb2, s5=s5: e.matmul(
                                ps_ld[0:16, tb2 * 512:(tb2 + 1) * 512], ldA_sb[sp_][:, ci, :],
                                x2s[:, ci, tb2 * 512:(tb2 + 1) * 512],
                                start=(s5 == 0 and ci == 0), stop=(ci == SC - 1)), inc)
                            if inc:
                                ld_mm_pe = r
                    ldA_pe_free[sp_] = ld_mm_pe
                    if S + 1 < NG * NS:
                        npar = (S + 1) % 2
                        dq_next = []
                        for ci in range(SC):
                            c2 = ((S + 1) % NS) * SC + ci
                            wt("g", "pe", wdn_free[npar])
                            dq_next.append(op("g", lambda e, c2=c2, ci=ci, npar=npar: e.dma_start(
                                wdn[npar][ci][:], wdnq_d[c2]), "wl", 16))
                    if s5 == 0:
                        acc_init = [None] * 8
                        for tcg in range(8):
                            if gi > 0:
                                wtt("g", out_dma[gi - 1][tcg])
                            acc_init[tcg] = op("g", lambda e, tcg=tcg, gcol=gcol: e.dma_start(
                                ACC[:, tcg, :], attn_scr[gcol + tcg * 128:gcol + (tcg + 1) * 128, :]), "wl", 16)
                    if s5 == NS - 1 and gi + 1 < NG:
                        xng_toks_next = [None] * 8
                        for tc8 in range(8):
                            tc = (gi + 1) * 8 + tc8
                            wt("y", "pe", la_pe_free)
                            xng_toks_next[tc8] = op("y", lambda e, tc=tc, tc8=tc8: e.dma_start(
                                xng[:, :, tc8 * 128:(tc8 + 1) * 128], xnT_scr[tc]), "hw", 16)
                        for cj in range(2):
                            slotj = s_use % 2
                            s_use += 1
                            wt("g", "dv", s_slot_user[slotj])
                            pre_us[cj] = (op("g", lambda e, cj=cj, slotj=slotj: e.dma_start(
                                s_stage[slotj][:].rearrange("p (k c) -> p k c", k=KD),
                                wups_d[cj]), "wl", 16), slotj)
                    relu_all = cnt["ac"]
                    dn_last = 0
                    for hf in range(2):
                        for tcg in range(8):
                            dp = (hf * 8 + tcg) % 2
                            wt("t", "ac", relu_all)
                            bk, bv = pdown_free[dp]
                            wt("t", bk, bv)
                            for ci in range(SC):
                                wt("t", "dv", deq_dn_arr[ci])
                                for nb2 in range(2):
                                    inc = "pe" if (ci == SC - 1 and nb2 == 1) else None
                                    r = op("t", lambda e, dp=dp, ci=ci, nb2=nb2, tcg=tcg, hf=hf, par=par: e.matmul(
                                        pdown[dp][:, nb2 * 512:(nb2 + 1) * 512],
                                        x2s[:, ci, tcg * 128:(tcg + 1) * 128],
                                        wdn[par][ci][:, hf * 1024 + nb2 * 512:hf * 1024 + (nb2 + 1) * 512],
                                        start=(ci == 0), stop=(ci == SC - 1 and nb2 == 1)), inc)
                                    if inc:
                                        dn_mm = r
                            dn_last = dn_mm
                            wt("v", "pe", dn_mm)
                            wtt("v", acc_init[tcg])
                            aad = op("v", lambda e, tcg=tcg, hf=hf, dp=dp: e.tensor_tensor(
                                ACC[:, tcg, hf * 1024:(hf + 1) * 1024],
                                ACC[:, tcg, hf * 1024:(hf + 1) * 1024],
                                pdown[dp][:], ALU.add), "dv")
                            pdown_free[dp] = ("dv", aad)
                    pe_x2_free = dn_last
                    wdn_free[par] = dn_last
                wt("v", "pe", ld_mm_pe)
                wt("v", "pe", la_pe_free)
                if gi > 0:
                    wt("v", "pe", ldb_pe_free)
                ldb_ev = op("v", lambda e: e.tensor_copy(ldb[:], ps_ld[0:16, :]), "dv")
                psld_free = ldb_ev
                if dbg and gi == 0:
                    wt("g", "ac", cnt["ac"])
                    op("g", lambda e: e.dma_start(dbg_x2, x2s[:]), "wl", 16)
                bd_adds = [[0, 0] for _ in range(8)]
                bd_prev_mm = ldb_pe_free
                for jh in range(2):
                    wt("g", "pe", bd_prev_mm)
                    bdld = op("g", lambda e, jh=jh: e.dma_start(
                        Bd_sb[:], ldB_d[:, jh * 1024:(jh + 1) * 1024]), "wl", 16)
                    for tcg in range(8):
                        dp = (jh * 8 + tcg) % 2
                        wt("t", "dv", ldb_ev)
                        wtt("t", bdld)
                        bk, bv = pdown_free[dp]
                        wt("t", bk, bv)
                        op("t", lambda e, dp=dp, tcg=tcg: e.matmul(
                            pdown[dp][:, 0:512], ldb[:, tcg * 128:(tcg + 1) * 128],
                            Bd_sb[:, 0:512], start=True, stop=False))
                        bd_mm = op("t", lambda e, dp=dp, tcg=tcg: e.matmul(
                            pdown[dp][:, 512:1024], ldb[:, tcg * 128:(tcg + 1) * 128],
                            Bd_sb[:, 512:1024], start=True, stop=True), "pe")
                        ldb_pe_free = bd_mm
                        wt("v", "pe", bd_mm)
                        bd_adds[tcg][jh] = op("v", lambda e, tcg=tcg, jh=jh, dp=dp: e.tensor_tensor(
                            ACC[:, tcg, jh * 1024:(jh + 1) * 1024],
                            ACC[:, tcg, jh * 1024:(jh + 1) * 1024],
                            pdown[dp][:], ALU.add), "dv")
                        pdown_free[dp] = ("dv", bd_adds[tcg][jh])
                    bd_prev_mm = ldb_pe_free
                for tcg in range(8):
                    wt("g", "dv", max(bd_adds[tcg]))
                    xa = op("g", lambda e, tcg=tcg, gcol=gcol: e.dma_start(
                        ACC[:, tcg, :], x_d[gcol + tcg * 128:gcol + (tcg + 1) * 128, :],
                        accum_op=ALU.add), "wl", 16)
                    wtt("y", xa)
                    out_dma[gi][tcg] = op("y", lambda e, tcg=tcg, gcol=gcol: e.dma_start(
                        out_d[gcol + tcg * 128:gcol + (tcg + 1) * 128, :],
                        ACC[:, tcg, :]), "out", 16)
            if dbg:
                wts("g", fence("wl"))
            wts("y", fence("out"))
            B.run(nc)

    return nc


def _prep_inputs(L0_attn_Wqkv, L0_attn_Wg, L0_attn_Wout, L0_attn_norm_w,
                 L0_ffn_w_up_q, L0_ffn_w_up_scale, L0_ffn_b_up,
                 L0_ffn_w_up_lora_a, L0_ffn_w_up_lora_b,
                 L0_ffn_w_down_q, L0_ffn_w_down_scale, L0_ffn_b_down,
                 L0_ffn_w_down_lora_a, L0_ffn_w_down_lora_b):
    C = np.ascontiguousarray
    f = lambda a: np.asarray(a, np.float32)
    wqkv_p = C(f(L0_attn_Wqkv).reshape(D, H, 384).transpose(1, 0, 2)
               .reshape(H, KD, 128, 384).transpose(0, 2, 1, 3))
    wg_p = C(f(L0_attn_Wg).reshape(KD, 128, H, 128).transpose(2, 1, 0, 3))
    wout_p = C(f(L0_attn_Wout).reshape(KD, 128, 4, 512).transpose(2, 1, 0, 3))
    nw_p = C(f(L0_attn_norm_w).reshape(H, 128).T)
    wupq_p = C(np.asarray(L0_ffn_w_up_q, np.int32).T
               .reshape(KD, 128, KH, 128).transpose(2, 1, 0, 3))
    wups_p = C(np.repeat(f(L0_ffn_w_up_scale), 64, axis=1).T
               .reshape(KD, 128, KH, 128).transpose(2, 1, 0, 3))
    bup_p = C(f(L0_ffn_b_up).reshape(KH, 128).T)
    laA_p = C(f(L0_ffn_w_up_lora_a).reshape(KD, 128, R).transpose(1, 0, 2))
    laB_p = C(f(L0_ffn_w_up_lora_b))
    wdnq_p = C(np.asarray(L0_ffn_w_down_q, np.int32).T.reshape(KH, 128, D))
    wdns_p = C(np.repeat(f(L0_ffn_w_down_scale), 64, axis=1).T.reshape(KH, 128, D))
    bdn_p = C(f(L0_ffn_b_down).reshape(1, D))
    ldA_p = C(f(L0_ffn_w_down_lora_a).reshape(KH, 128, R).transpose(1, 0, 2))
    ldB_p = C(f(L0_ffn_w_down_lora_b))
    return dict(wqkv=wqkv_p, wg=wg_p, wout=wout_p, nw=nw_p, wupq=wupq_p,
                wups=wups_p, bup=bup_p, laA=laA_p, laB=laB_p, wdnq=wdnq_p,
                wdns=wdns_p, bdn=bdn_p, ldA=ldA_p, ldB=ldB_p)


_NC_CACHE = {}


def kernel(L0_attn_x, L0_attn_Wqkv, L0_attn_Wg, L0_attn_Wout, L0_attn_norm_w,
           L0_ffn_w_up_q, L0_ffn_w_up_scale, L0_ffn_b_up, L0_ffn_w_up_lora_a,
           L0_ffn_w_up_lora_b, L0_ffn_w_down_q, L0_ffn_w_down_scale,
           L0_ffn_b_down, L0_ffn_w_down_lora_a, L0_ffn_w_down_lora_b):
    if "nc" not in _NC_CACHE:
        _NC_CACHE["nc"] = build_nc(dbg=False)
    nc = _NC_CACHE["nc"]
    shared = _prep_inputs(L0_attn_Wqkv, L0_attn_Wg, L0_attn_Wout, L0_attn_norm_w,
                          L0_ffn_w_up_q, L0_ffn_w_up_scale, L0_ffn_b_up,
                          L0_ffn_w_up_lora_a, L0_ffn_w_up_lora_b,
                          L0_ffn_w_down_q, L0_ffn_w_down_scale, L0_ffn_b_down,
                          L0_ffn_w_down_lora_a, L0_ffn_w_down_lora_b)
    x = np.asarray(L0_attn_x, np.float32)
    in_maps = []
    for i in range(8):
        m = dict(shared)
        m["x"] = np.ascontiguousarray(x[i])
        in_maps.append(m)
    res = run_bass_kernel_spmd(nc, in_maps, core_ids=list(range(8)))
    return np.stack([r["out"] for r in res.results], axis=0)


# version 26
# speedup vs baseline: 1.0201x; 1.0061x over previous
import math
import numpy as np
import ml_dtypes
from contextlib import ExitStack

import concourse.bass as bass
import concourse.mybir as mybir
from concourse.bass_utils import run_bass_kernel_spmd

F32 = mybir.dt.float32
BF16 = mybir.dt.bfloat16
I32 = mybir.dt.int32
AF = mybir.ActivationFunctionType
ALU = mybir.AluOpType
AX = mybir.AxisListType

H, DH, BLK, NT, D, HH, R = 16, 128, 256, 2048, 2048, 8192, 16
NB = NT // BLK
KD = D // 128
KH = HH // 128
NTC = NT // 128
NG = 2
TG = NT // NG
NS = 8
SC = 8
EPS_LN, EPS_RMS = 1e-5, 1e-6

BF = ml_dtypes.bfloat16


def _slopes():
    start = 2.0 ** (-(2.0 ** -(math.log2(H) - 3)))
    return np.array([start * start ** i for i in range(H)], dtype=np.float64)


def _consts():
    s = _slopes()
    n = np.arange(128)[:, None]
    m = np.arange(256)[None, :]
    maskc = np.zeros((128, H, 384), dtype=np.float32)
    qdbc = np.zeros((128, H, 256), dtype=np.float32)
    kdec = np.zeros((128, H, 2), dtype=np.float32)
    for h in range(H):
        mm_ = np.exp(-s[h] * (m - n)) * (m >= n)
        maskc[:, h, 0:256] = mm_.astype(np.float32)
        maskc[:, h, 256:384] = mm_[:, 0:128].astype(np.float32)
        qdbc[:, h, :] = np.exp(-s[h] * (np.arange(256) + 1.0))[None, :].astype(np.float32)
        for c in range(2):
            kdec[:, h, c] = np.exp(-s[h] * (255.0 - (c * 128 + np.arange(128)))).astype(np.float32)
    bdec = [float(np.exp(-s[h] * BLK)) for h in range(H)]
    return maskc, qdbc.astype(BF), kdec, bdec


class _ES:

    def __init__(self):
        self.q = {k: [] for k in "tvsgy"}

    def add(self, ek, fn):
        self.q[ek].append(fn)

    def run(self, nc):
        def mk(lst):
            def body(e):
                for f in lst:
                    f(e)
            return body
        with nc.Block() as blk:
            if self.q["g"]:
                blk.gpsimd(mk(self.q["g"]))
            if self.q["y"]:
                blk.sync(mk(self.q["y"]))
            if self.q["t"]:
                blk.tensor(mk(self.q["t"]))
            if self.q["v"]:
                blk.vector(mk(self.q["v"]))
            if self.q["s"]:
                blk.scalar(mk(self.q["s"]))
        self.q = {k: [] for k in "tvsgy"}


def build_nc(dbg=False):
    nc = bass.Bass("TRN2", target_bir_lowering=False)

    x_d = nc.dram_tensor("x", [NT, D], F32, kind="ExternalInput").ap()
    wqkv_d = nc.dram_tensor("wqkv", [H, 128, KD, 384], F32, kind="ExternalInput").ap()
    wg_d = nc.dram_tensor("wg", [H, 128, KD, 128], F32, kind="ExternalInput").ap()
    wout_d = nc.dram_tensor("wout", [4, 128, KD, 512], F32, kind="ExternalInput").ap()
    nw_d = nc.dram_tensor("nw", [128, H], F32, kind="ExternalInput").ap()
    wupq_d = nc.dram_tensor("wupq", [KH, 128, KD, 128], I32, kind="ExternalInput").ap()
    wups_d = nc.dram_tensor("wups", [KH, 128, KD, 128], F32, kind="ExternalInput").ap()
    bup_d = nc.dram_tensor("bup", [128, KH], F32, kind="ExternalInput").ap()
    laA_d = nc.dram_tensor("laA", [128, KD, R], F32, kind="ExternalInput").ap()
    laB_d = nc.dram_tensor("laB", [R, HH], F32, kind="ExternalInput").ap()
    wdnq_d = nc.dram_tensor("wdnq", [KH, 128, D], I32, kind="ExternalInput").ap()
    wdns_d = nc.dram_tensor("wdns", [KH, 128, D], F32, kind="ExternalInput").ap()
    bdn_d = nc.dram_tensor("bdn", [1, D], F32, kind="ExternalInput").ap()
    ldA_d = nc.dram_tensor("ldA", [128, KH, R], F32, kind="ExternalInput").ap()
    ldB_d = nc.dram_tensor("ldB", [R, D], F32, kind="ExternalInput").ap()
    out_d = nc.dram_tensor("out", [NT, D], F32, kind="ExternalOutput").ap()

    attn_scr = nc.dram_tensor("attn_scr", [NT, D], BF16, kind="Internal").ap()
    xnT_scr = nc.dram_tensor("xnT_scr", [NTC, 128, KD, 128], BF16, kind="Internal").ap()
    scr_r = nc.dram_tensor("scr_r", [16, 128], F32, kind="Internal").ap()

    maskc_np, qdbc_np, kdec_np, bdec = _consts()
    maskc_d = nc.inline_tensor(maskc_np.astype(BF), "c_maskc")
    qdbc_d = nc.inline_tensor(qdbc_np, "c_qdbc")
    kdec_d = nc.inline_tensor(kdec_np, "c_kdec")
    identb_d = nc.inline_tensor(np.eye(128, dtype=BF), "c_identb")
    identf_d = nc.inline_tensor(np.eye(128, dtype=np.float32), "c_identf")

    if dbg:
        dbg_xnT = nc.dram_tensor("dbg_xnT", [128, KD, NT], F32, kind="ExternalOutput").ap()
        dbg_OT = nc.dram_tensor("dbg_OT", [128, H, NT], F32, kind="ExternalOutput").ap()
        dbg_og = nc.dram_tensor("dbg_og", [128, H, NT], F32, kind="ExternalOutput").ap()
        dbg_attn = nc.dram_tensor("dbg_attn", [NT, D], F32, kind="ExternalOutput").ap()
        dbg_r = nc.dram_tensor("dbg_r", [128, 48], F32, kind="ExternalOutput").ap()
        dbg_x2 = nc.dram_tensor("dbg_x2", [128, SC, TG], F32, kind="ExternalOutput").ap()

    cnt = {"pe": 0, "dv": 0, "ac": 0}
    NLANES = {"wl": 8, "hw": 4, "out": 4}

    with ExitStack() as S0:
        E0 = S0.enter_context
        sem = {
            "pe": E0(nc.semaphore("s_pe")), "dv": E0(nc.semaphore("s_dv")),
            "ac": E0(nc.semaphore("s_ac")),
        }
        for _cls, _nl in NLANES.items():
            for _i in range(_nl):
                _k = f"{_cls}{_i}"
                sem[_k] = E0(nc.semaphore("s_" + _k))
                cnt[_k] = 0
        dma_idx = {"wl": 0, "hw": 0, "out": 0}
        B = _ES()

        def wt(ek, sk, val):
            if val:
                B.add(ek, lambda e, s=sem[sk], v=val: e.wait_ge(s, v))

        def wtt(ek, tok):
            if tok:
                wt(ek, tok[0], tok[1])

        def wts(ek, toks):
            for t in toks:
                wtt(ek, t)

        def fence(cls):
            return [(f"{cls}{i}", cnt[f"{cls}{i}"]) for i in range(NLANES[cls])
                    if cnt[f"{cls}{i}"]]

        def op(ek, fn, inc=None, n=1):
            if inc is None:
                B.add(ek, fn)
                return None
            if inc in NLANES:
                lane = dma_idx[inc] % NLANES[inc]
                dma_idx[inc] += 1
                key = f"{inc}{lane}"
                cnt[key] += 16
                tok = (key, cnt[key])
                B.add(ek, lambda e, fn=fn, s=sem[key]: fn(e).then_inc(s, 16))
                return tok
            cnt[inc] += n
            tgt = cnt[inc]
            B.add(ek, lambda e, fn=fn, s=sem[inc], n=n: fn(e).then_inc(s, n))
            return tgt

        epsb = E0(nc.sbuf_tensor("epsb", [128, 2], F32))
        identb = E0(nc.sbuf_tensor("identb", [128, 128], BF16))
        ones_col = E0(nc.sbuf_tensor("ones_col", [128, 1], BF16))
        ones1b = E0(nc.sbuf_tensor("ones1b", [1, 128], BF16))

        with ExitStack() as S1:
            E1 = S1.enter_context
            xnT = E1(nc.sbuf_tensor("xnT", [128, KD, NT], BF16))
            OT = E1(nc.sbuf_tensor("OT", [128, H, NT], BF16))
            ps_rms = E1(nc.psum_tensor("ps_rms", [128, 16], F32))
            pb = [E1(nc.psum_tensor(f"pb{i}", [128, 512], F32)) for i in range(4)]
            TR = [E1(nc.psum_tensor(f"ptr{i}", [128, 512], BF16)) for i in range(2)]
            psK = E1(nc.psum_tensor("psK", [128, 256], F32))

            with ExitStack() as SP:
                E = SP.enter_context
                xt = [E(nc.sbuf_tensor(f"xt{i}", [128, D], F32)) for i in range(4)]
                xsq = [E(nc.sbuf_tensor(f"xsq{i}", [128, D], BF16)) for i in range(4)]
                xn_tok = [E(nc.sbuf_tensor(f"xntok{i}", [128, D], BF16)) for i in range(4)]
                st = E(nc.sbuf_tensor("lnst", [128, 4, 8], F32))

                idb_tok = op("y", lambda e: e.dma_start(identb[:], identb_d.ap()), "hw", 16)
                op("v", lambda e: e.memset(ones_col[:], 1.0), "dv")
                op("v", lambda e: e.memset(ones1b[:], 1.0), "dv")
                op("v", lambda e: e.memset(epsb[:, 0:1], EPS_LN), "dv")
                epsv = op("v", lambda e: e.memset(epsb[:, 1:2], EPS_RMS), "dv")

                apply_inc = [0] * NTC
                xnw_toks = [None] * NTC
                sqd_inc = [0] * NTC
                e2_arr = [0] * NTC
                trev_inc = [0] * (NTC * 4)
                trmm_inc = [0] * (NTC * 4)
                xld = [None] * NTC
                for tp_ in range(NTC // 4):
                    tcs = tuple(4 * tp_ + j_ for j_ in range(4))
                    e1 = [0] * 4
                    e2 = [0] * 4
                    e3 = [0] * 4
                    e4 = [0] * 4
                    e5 = [0] * 4
                    e6 = [0] * 4
                    e7 = [0] * 4
                    for i, tc in enumerate(tcs):
                        ii = tc % 4
                        if tc >= 4:
                            wt("y", "dv", apply_inc[tc - 4])
                            wt("y", "ac", sqd_inc[tc - 4])
                        xld[tc] = op("y", lambda e, tc=tc, ii=ii: e.dma_start(
                            xt[ii][:], x_d[tc * 128:(tc + 1) * 128, :]), "hw", 16)
                    for i, tc in enumerate(tcs):
                        ii = tc % 4
                        wtt("v", xld[tc])
                        e1[i] = op("v", lambda e, i=i, ii=ii: e.reduce_sum(st[:, i, 0:1], xt[ii][:], AX.X), "dv")
                    for i, tc in enumerate(tcs):
                        ii = tc % 4
                        wtt("s", xld[tc])
                        if tc >= 4:
                            wt("s", "dv", e2_arr[tc - 4])
                        sqd_inc[tc] = op("s", lambda e, i=i, ii=ii: e.activation(
                            xsq[i][:], xt[ii][:], AF.Square), "ac")
                    for i, tc in enumerate(tcs):
                        wt("v", "ac", sqd_inc[tc])
                        e2[i] = op("v", lambda e, i=i: e.reduce_sum(st[:, i, 1:2], xsq[i][:], AX.X), "dv")
                        e2_arr[tc] = e2[i]
                    for i, tc in enumerate(tcs):
                        wt("v", "dv", e2[i])
                        e3[i] = op("v", lambda e, i=i: e.tensor_scalar(
                            st[:, i, 2:4], st[:, i, 0:2], 1.0 / D, None, ALU.mult), "dv")
                    for i, tc in enumerate(tcs):
                        wt("v", "dv", e3[i])
                        e4[i] = op("v", lambda e, i=i: e.tensor_tensor(
                            st[:, i, 4:5], st[:, i, 2:3], st[:, i, 2:3], ALU.mult), "dv")
                    for i, tc in enumerate(tcs):
                        wt("v", "dv", e4[i])
                        e5[i] = op("v", lambda e, i=i: e.tensor_tensor(
                            st[:, i, 5:6], st[:, i, 3:4], st[:, i, 4:5], ALU.subtract), "dv")
                    for i, tc in enumerate(tcs):
                        wt("s", "dv", e5[i])
                        if tp_ == 0 and i == 0:
                            wt("s", "dv", epsv)
                        e6[i] = op("s", lambda e, i=i: e.activation(
                            st[:, i, 6:7], st[:, i, 5:6], AF.Sqrt, bias=epsb[:, 0:1]), "ac")
                    for i, tc in enumerate(tcs):
                        wt("v", "ac", e6[i])
                        e7[i] = op("v", lambda e, i=i: e.reciprocal(st[:, i, 7:8], st[:, i, 6:7]), "dv")
                    for i, tc in enumerate(tcs):
                        wt("v", "dv", e7[i])
                        if tc >= 4:
                            wt("v", "pe", trmm_inc[(tc - 4) * 4 + 3])
                        apply_inc[tc] = op("v", lambda e, i=i, ii=tc % 4: e.tensor_scalar(
                            xn_tok[i][:], xt[ii][:], st[:, i, 2:3], st[:, i, 7:8],
                            ALU.subtract, ALU.mult), "dv")
                    for i, tc in enumerate(tcs):
                        for grp in range(4):
                            gidx = tc * 4 + grp
                            tp2 = gidx % 2
                            wt("t", "dv", apply_inc[tc])
                            if gidx >= 2:
                                wt("t", "dv", trev_inc[gidx - 2])
                            if gidx == 0:
                                wtt("t", idb_tok)
                            for j in range(4):
                                k = grp * 4 + j
                                inc = "pe" if j == 3 else None
                                r = op("t", lambda e, tp2=tp2, j=j, k=k, i=i: e.transpose(
                                    TR[tp2][:, j * 128:(j + 1) * 128],
                                    xn_tok[i][:, k * 128:(k + 1) * 128], identb[:]), inc)
                                if j == 3:
                                    trmm_inc[gidx] = r
                            wt("v", "pe", trmm_inc[gidx])
                            trev_inc[gidx] = op("v", lambda e, tp2=tp2, tc=tc, grp=grp: e.tensor_copy(
                                xnT[:, grp * 4:(grp + 1) * 4, tc * 128:(tc + 1) * 128],
                                TR[tp2][:].rearrange("p (j c) -> p j c", j=4)), "dv")
                        wt("g", "dv", trev_inc[tc * 4 + 3])
                        xnw_toks[tc] = op("g", lambda e, tc=tc: e.dma_start(
                            xnT_scr[tc], xnT[:, :, tc * 128:(tc + 1) * 128]), "wl", 16)
                B.run(nc)

            if dbg:
                with ExitStack() as SP:
                    op("g", lambda e: e.dma_start(dbg_xnT, xnT[:]), "wl", 16)
                    wts("g", fence("wl"))
                    B.run(nc)

            with ExitStack() as SP:
                E = SP.enter_context
                whb = [E(nc.sbuf_tensor(f"whb{i}", [128, KD, 384], BF16)) for i in range(2)]
                qkv = [E(nc.sbuf_tensor(f"qkv{i}", [128, 3, NT], BF16)) for i in range(2)]
                maskp = [E(nc.sbuf_tensor(f"maskp{i}", [128, 2, 384], BF16)) for i in range(2)]
                qdbp = [E(nc.sbuf_tensor(f"qdbp{i}", [128, 2, 256], BF16)) for i in range(2)]
                kdec_sb = E(nc.sbuf_tensor("kdec", [128, H, 2], F32))
                Sd = [E(nc.sbuf_tensor(f"Sd{i}", [128, 384], BF16)) for i in range(2)]
                vtok = [E(nc.sbuf_tensor(f"vtok{i}", [128, 2, 2, 128], BF16)) for i in range(2)]
                kdt = [E(nc.sbuf_tensor(f"kdt{i}", [128, 2, 2, 128], BF16)) for i in range(2)]
                qd = [E(nc.sbuf_tensor(f"qd{i}", [128, 256], BF16)) for i in range(2)]
                sqb = E(nc.sbuf_tensor("sqb", [128, 2, NB, 256], BF16))
                kv2 = [E(nc.sbuf_tensor(f"kv{i}", [128, 128], F32)) for i in range(2)]
                kvbf2 = [E(nc.sbuf_tensor(f"kvbf{i}", [128, 128], BF16)) for i in range(2)]
                wgh = [E(nc.sbuf_tensor("wgh0", [128, KD, 2, 128], BF16))] * 2
                g_sb = [E(nc.sbuf_tensor(f"gsb{i}", [128, 512], BF16)) for i in range(2)]
                ogt = g_sb
                nw_sb = E(nc.sbuf_tensor("nwsb", [128, H], F32))

                kdl = op("y", lambda e: e.dma_start(kdec_sb[:], kdec_d.ap()), "hw", 16)
                nwl = op("y", lambda e: e.dma_start(nw_sb[:], nw_d), "hw", 16)

                whb_pe_free = [0, 0]
                qkv_pe_last = [0, 0]
                qkv_dv_last = [0, 0]
                silu_done = [0, 0]
                bank_free = {i: ("ac", 0) for i in range(4)}
                trbank_free = [None, None]
                sd_pe_free = [0, 0]
                vk_pe_free = [[0, 0], [0, 0]]
                psK_free = [0, 0]
                pbO_free = [("dv", 0), ("dv", 0)]
                pbS_free = [("dv", 0), ("dv", 0)]
                kvbf_src = [("dv", 0), ("dv", 0)]
                last_kvc = [0, 0]
                mask_dv_free = [0, 0]
                pending_rms = None
                rms_pe_last = 0
                first_rms = [True]
                wgh_pe_free = [0, 0]
                ogpair_dv_free = [0, 0]
                og_dv_last = 0

                def emit_rms(sq_last):
                    wt("t", "dv", sq_last)
                    r = 0
                    for p_ in range(2):
                        for b in range(NB):
                            for mc in range(2):
                                fr = first_rms[0]
                                first_rms[0] = False
                                inc = "pe" if (p_ == 1 and b == NB - 1 and mc == 1) else None
                                r = op("t", lambda e, p_=p_, b=b, mc=mc, fr=fr: e.matmul(
                                    ps_rms[:, 2 * b + mc:2 * b + mc + 1],
                                    sqb[:, p_, b, mc * 128:(mc + 1) * 128], ones_col[:],
                                    start=fr, stop=True), inc) or r
                    return r

                for hp in range(H // 2):
                    pp = hp % 2
                    heads = (2 * hp, 2 * hp + 1)
                    wt("y", "dv", mask_dv_free[pp])
                    mtok = op("y", lambda e, hp=hp, pp=pp: e.dma_start(
                        maskp[pp][:], maskc_d.ap()[:, 2 * hp:2 * hp + 2, :]), "hw", 16)
                    qtok = op("y", lambda e, hp=hp, pp=pp: e.dma_start(
                        qdbp[pp][:], qdbc_d.ap()[:, 2 * hp:2 * hp + 2, :]), "hw", 16)
                    for p_ in range(2):
                        h = heads[p_]
                        wt("g", "pe", whb_pe_free[p_])
                        wq = op("g", lambda e, h=h, p_=p_: e.dma_start(
                            whb[p_][:], wqkv_d[h]), "wl", 16)
                        for f in range(3):
                            for tb in range(4):
                                rot = (f * 4 + tb) % 4
                                wtt("t", wq)
                                bk, bv = bank_free[rot]
                                wt("t", bk, bv)
                                if rot < 2:
                                    bk2, bv2 = pbS_free[rot]
                                    wt("t", bk2, bv2)
                                else:
                                    bk2, bv2 = pbO_free[rot - 2]
                                    wt("t", bk2, bv2)
                                mmv = None
                                for k in range(KD):
                                    inc = "pe" if k == KD - 1 else None
                                    mmv = op("t", lambda e, p_=p_, k=k, f=f, tb=tb, rot=rot: e.matmul(
                                        pb[rot][:], whb[p_][:, k, f * 128:(f + 1) * 128],
                                        xnT[:, k, tb * 512:(tb + 1) * 512],
                                        start=(k == 0), stop=(k == KD - 1)), inc) or mmv
                                whb_pe_free[p_] = mmv
                                wt("s", "pe", mmv)
                                if hp > 0:
                                    wt("s", "pe", qkv_pe_last[p_])
                                    wt("s", "dv", qkv_dv_last[p_])
                                sv = op("s", lambda e, p_=p_, f=f, tb=tb, rot=rot: e.activation(
                                    qkv[p_][:, f, tb * 512:(tb + 1) * 512], pb[rot][:], AF.Silu), "ac")
                                bank_free[rot] = ("ac", sv)
                                silu_done[p_] = sv
                        wt("v", "dv", last_kvc[p_])
                        op("v", lambda e, p_=p_: e.memset(kv2[p_][:], 0.0), "dv")
                        kvi = op("v", lambda e, p_=p_: e.memset(kvbf2[p_][:], 0.0), "dv")
                        kvbf_src[p_] = ("dv", kvi)
                    if pending_rms is not None:
                        rms_pe_last = emit_rms(pending_rms)
                        pending_rms = None
                    oev = [[0] * NB, [0] * NB]
                    tr7 = [0, 0]
                    pending_kv = None

                    def emit_kvchain(p2, h2, blc2, o_mm2, kvd_mm2, b2):
                        wt("v", "pe", o_mm2)
                        oev[p2][b2] = op("v", lambda e, p2=p2, h2=h2, blc2=blc2: e.tensor_copy(
                            OT[:, h2, blc2:blc2 + 256], pb[2 + p2][:, 0:256]), "dv")
                        pbO_free[p2] = ("dv", oev[p2][b2])
                        wt("v", "pe", kvd_mm2)
                        psK_free[p2] = op("v", lambda e, p2=p2, h2=h2: e.scalar_tensor_tensor(
                            kv2[p2][:], kv2[p2][:], bdec[h2], psK[:, p2 * 128:(p2 + 1) * 128],
                            ALU.mult, ALU.add), "dv")
                        if b2 < NB - 1:
                            wt("v", "dv", psK_free[p2])
                            kvc = op("v", lambda e, p2=p2: e.tensor_copy(
                                kvbf2[p2][:], kv2[p2][:]), "dv")
                            kvbf_src[p2] = ("dv", kvc)
                            last_kvc[p2] = kvc
                    for b in range(NB):
                        blc = b * 256
                        for p_ in range(2):
                            h = heads[p_]
                            wt("t", "ac", silu_done[p_])
                            bk, bv = pbS_free[p_]
                            wt("t", bk, bv)
                            op("t", lambda e, p_=p_, blc=blc: e.matmul(
                                pb[p_][:, 0:256], qkv[p_][:, 1, blc:blc + 128],
                                qkv[p_][:, 0, blc:blc + 256], start=True, stop=False))
                            s_mm = op("t", lambda e, p_=p_, blc=blc: e.matmul(
                                pb[p_][:, 256:384], qkv[p_][:, 1, blc + 128:blc + 256],
                                qkv[p_][:, 0, blc + 128:blc + 256], start=False, stop=True), "pe")
                            if trbank_free[p_]:
                                wt("t", trbank_free[p_][0], trbank_free[p_][1])
                            tr_half = 0
                            for j, (fch, off) in enumerate(((2, 0), (2, 128), (1, 0), (1, 128))):
                                inc = "pe" if j in (1, 3) else None
                                r = op("t", lambda e, p_=p_, j=j, fch=fch, off=off, blc=blc: e.transpose(
                                    TR[p_][:, j * 128:(j + 1) * 128],
                                    qkv[p_][:, fch, blc + off:blc + off + 128], identb[:]), inc)
                                if j == 1:
                                    tr_half = r
                                if j == 3:
                                    tr_mm = r
                            tr7[p_] = tr_mm
                            if b == 0:
                                wtt("v", mtok)
                                wtt("v", qtok)
                                if hp == 0 and p_ == 0:
                                    wtt("v", kdl)
                            wt("v", "pe", s_mm)
                            wt("v", "pe", sd_pe_free[p_])
                            sd_done = op("v", lambda e, p_=p_, pp=pp: e.tensor_tensor(
                                Sd[p_][:, 0:384], pb[p_][:, 0:384], maskp[pp][:, p_, :], ALU.mult), "dv")
                            pbS_free[p_] = ("dv", sd_done)
                            qd_done = op("v", lambda e, p_=p_, pp=pp, blc=blc: e.tensor_tensor(
                                qd[p_][:], qkv[p_][:, 0, blc:blc + 256],
                                qdbp[pp][:, p_, :], ALU.mult), "dv")
                            qkv_dv_last[p_] = qd_done
                            mask_dv_free[pp] = qd_done
                            if pending_kv is not None:
                                emit_kvchain(*pending_kv)
                                pending_kv = None
                            sp = b % 2
                            wt("s", "pe", tr_half)
                            wt("s", "pe", vk_pe_free[p_][sp])
                            if b == 0 and hp == 0 and p_ == 0:
                                wtt("s", kdl)
                            op("s", lambda e, p_=p_, sp=sp: e.activation(vtok[p_][:, sp, 0, :], TR[p_][:, 0:128], AF.Copy), "ac")
                            op("s", lambda e, p_=p_, sp=sp: e.activation(vtok[p_][:, sp, 1, :], TR[p_][:, 128:256], AF.Copy), "ac")
                            wt("s", "pe", tr_mm)
                            op("s", lambda e, p_=p_, h=h, sp=sp: e.activation(
                                kdt[p_][:, sp, 0, :], TR[p_][:, 256:384], AF.Copy, scale=kdec_sb[:, h, 0:1]), "ac")
                            vk_done = op("s", lambda e, p_=p_, h=h, sp=sp: e.activation(
                                kdt[p_][:, sp, 1, :], TR[p_][:, 384:512], AF.Copy, scale=kdec_sb[:, h, 1:2]), "ac")
                            trbank_free[p_] = ("ac", vk_done)
                            wt("t", "dv", qd_done)
                            wt("t", "ac", vk_done)
                            op("t", lambda e, p_=p_: e.matmul(
                                pb[2 + p_][:, 0:256], kvbf2[p_][:], qd[p_][:], start=True, stop=False))
                            op("t", lambda e, p_=p_, sp=sp: e.matmul(
                                pb[2 + p_][:, 0:256], vtok[p_][:, sp, 0, :], Sd[p_][:, 0:256],
                                start=False, stop=False))
                            o_mm = op("t", lambda e, p_=p_, sp=sp: e.matmul(
                                pb[2 + p_][:, 128:256], vtok[p_][:, sp, 1, :], Sd[p_][:, 256:384],
                                start=False, stop=True), "pe")
                            sd_pe_free[p_] = o_mm
                            op("t", lambda e, p_=p_, sp=sp: e.matmul(
                                psK[:, p_ * 128:(p_ + 1) * 128],
                                kdt[p_][:, sp, 0, :], vtok[p_][:, sp, 0, :], start=(p_ == 0), stop=False))
                            kvd_mm = op("t", lambda e, p_=p_, sp=sp: e.matmul(
                                psK[:, p_ * 128:(p_ + 1) * 128],
                                kdt[p_][:, sp, 1, :], vtok[p_][:, sp, 1, :], start=False, stop=True), "pe")
                            vk_pe_free[p_][sp] = kvd_mm
                            pending_kv = (p_, h, blc, o_mm, kvd_mm, b)
                    if pending_kv is not None:
                        emit_kvchain(*pending_kv)
                        pending_kv = None
                    qkv_pe_last = [tr7[0], tr7[1]]
                    sq_last = 0
                    for p_ in range(2):
                        for b in range(NB):
                            wt("v", "dv", oev[p_][b])
                            if p_ == 0 and b == 0:
                                wt("v", "pe", rms_pe_last)
                            sq_last = op("v", lambda e, p_=p_, b=b, h=heads[p_]: e.tensor_tensor(
                                sqb[:, p_, b, :], OT[:, h, b * 256:b * 256 + 256],
                                OT[:, h, b * 256:b * 256 + 256], ALU.mult), "dv")
                    pending_rms = sq_last
                    wgl = [None, None]
                    for p_ in range(2):
                        h = heads[p_]
                        wt("g", "pe", max(wgh_pe_free))
                        wgl[p_] = op("g", lambda e, h=h, p_=p_, pp=pp: e.dma_start(
                            wgh[pp][:, :, p_, :], wg_d[h]), "wl", 16)
                    for p_ in range(2):
                        h = heads[p_]
                        for tb in range(4):
                            rot = (p_ * 4 + tb) % 4
                            r2 = (p_ * 4 + tb) % 2
                            wtt("t", wgl[p_])
                            bk, bv = bank_free[rot]
                            wt("t", bk, bv)
                            if rot < 2:
                                bk2, bv2 = pbS_free[rot]
                                wt("t", bk2, bv2)
                            else:
                                bk2, bv2 = pbO_free[rot - 2]
                                wt("t", bk2, bv2)
                            mmv = None
                            for k in range(KD):
                                inc = "pe" if k == KD - 1 else None
                                mmv = op("t", lambda e, pp=pp, p_=p_, k=k, tb=tb, rot=rot: e.matmul(
                                    pb[rot][:], wgh[pp][:, k, p_, :],
                                    xnT[:, k, tb * 512:(tb + 1) * 512],
                                    start=(k == 0), stop=(k == KD - 1)), inc) or mmv
                            wgh_pe_free[pp] = mmv
                            wt("s", "pe", mmv)
                            wt("s", "dv", ogpair_dv_free[r2])
                            g_ev = op("s", lambda e, rot=rot, r2=r2: e.activation(
                                g_sb[r2][:], pb[rot][:], AF.Sigmoid), "ac")
                            bank_free[rot] = ("ac", g_ev)
                            wt("v", "ac", g_ev)
                            if hp == 0 and p_ == 0 and tb == 0:
                                wtt("v", nwl)
                            t1 = op("v", lambda e, r2=r2, h=h: e.tensor_scalar(
                                g_sb[r2][:], g_sb[r2][:], nw_sb[:, h:h + 1], None, ALU.mult), "dv")
                            wt("v", "dv", t1)
                            wt("v", "dv", pending_rms)
                            og_dv_last = op("v", lambda e, r2=r2, h=h, tb=tb: e.tensor_tensor(
                                OT[:, h, tb * 512:(tb + 1) * 512],
                                OT[:, h, tb * 512:(tb + 1) * 512], ogt[r2][:], ALU.mult), "dv")
                            ogpair_dv_free[r2] = og_dv_last
                rms_pe_last = emit_rms(pending_rms)
                B.run(nc)

            if dbg:
                with ExitStack() as SP:
                    op("g", lambda e: e.dma_start(dbg_OT, OT[:]), "wl", 16)
                    wts("g", fence("wl"))
                    B.run(nc)

            with ExitStack() as SP:
                E = SP.enter_context
                wstage = [E(nc.sbuf_tensor(f"wstg{i}", [128, KD, 512], BF16)) for i in range(2)]
                astage = [E(nc.sbuf_tensor(f"astg{i}", [128, 512], BF16)) for i in range(2)]
                rms_sb = E(nc.sbuf_tensor("rms", [128, 48], F32))
                bbc = E(nc.sbuf_tensor("bbc", [128, D], BF16))
                bdn_sb = E(nc.sbuf_tensor("bdnsb", [1, D], BF16))

                bdl = op("g", lambda e: e.dma_start(bdn_sb[:], bdn_d), "wl", 16)

                r1 = op("v", lambda e: e.tensor_scalar(
                    rms_sb[:, 0:16], ps_rms[:], 1.0 / D, None, ALU.mult), "dv")
                wt("s", "dv", r1)
                r2 = op("s", lambda e: e.activation(
                    rms_sb[:, 16:32], rms_sb[:, 0:16], AF.Sqrt, bias=epsb[:, 1:2]), "ac")
                wt("v", "ac", r2)
                r3 = op("v", lambda e: e.reciprocal(rms_sb[:, 32:48], rms_sb[:, 16:32]), "dv")
                bbc_ev = 0
                for j in range(4):
                    wtt("t", bdl)
                    wt("t", "dv", bbc_ev)
                    r8 = op("t", lambda e, j=j: e.matmul(
                        pb[0][:], ones1b[:], bdn_sb[0:1, j * 512:(j + 1) * 512],
                        start=True, stop=True), "pe")
                    wt("v", "pe", r8)
                    bbc_ev = op("v", lambda e, j=j: e.tensor_copy(
                        bbc[:, j * 512:(j + 1) * 512], pb[0][:]), "dv")
                if dbg:
                    wt("g", "dv", r3)
                    op("g", lambda e: e.dma_start(dbg_r, rms_sb[:]), "wl", 16)
                    op("g", lambda e: e.dma_start(dbg_og, OT[:]), "wl", 16)

                wstage_pe_free = [0, 0]
                bank_free = {i: ("dv", bbc_ev if i == 0 else 0) for i in range(4)}
                astage_free = [None, None]
                nit = 0
                for st4 in range(4):
                    p_ = st4 % 2
                    wt("g", "pe", wstage_pe_free[p_])
                    wo_ld = op("g", lambda e, st4=st4, p_=p_: e.dma_start(
                        wstage[p_][:], wout_d[st4]), "wl", 16)
                    for tc in range(NTC):
                        rot = nit % 4
                        r2p = nit % 2
                        wtt("t", wo_ld)
                        bk, bv = bank_free[rot]
                        wt("t", bk, bv)
                        w_mm = None
                        for k in range(KD):
                            inc = "pe" if k == KD - 1 else None
                            w_mm = op("t", lambda e, p_=p_, k=k, tc=tc, rot=rot: e.matmul(
                                pb[rot][:], OT[:, k, tc * 128:(tc + 1) * 128],
                                wstage[p_][:, k, :], start=(k == 0), stop=(k == KD - 1)),
                                inc) or w_mm
                        wstage_pe_free[p_] = w_mm
                        wt("v", "pe", w_mm)
                        wtt("v", astage_free[r2p])
                        if nit == 0:
                            wt("v", "dv", r3)
                            wt("v", "dv", bbc_ev)
                        a_ev = op("v", lambda e, rot=rot, r2p=r2p, tc=tc, st4=st4: e.scalar_tensor_tensor(
                            astage[r2p][:], pb[rot][:], rms_sb[:, 32 + tc:33 + tc],
                            bbc[:, st4 * 512:(st4 + 1) * 512], ALU.mult, ALU.add), "dv")
                        bank_free[rot] = ("dv", a_ev)
                        wt("y", "dv", a_ev)
                        astage_free[r2p] = op("y", lambda e, r2p=r2p, tc=tc, st4=st4: e.dma_start(
                            attn_scr[tc * 128:(tc + 1) * 128, st4 * 512:(st4 + 1) * 512],
                            astage[r2p][:]), "hw", 16)
                        nit += 1
                attn_wr_done = fence("hw")
                p34_pe = cnt["pe"]
                B.run(nc)

            if dbg:
                with ExitStack() as SP:
                    wts("g", attn_wr_done)
                    op("g", lambda e: e.dma_start(dbg_attn, attn_scr), "wl", 16)
                    wts("g", fence("wl"))
                    B.run(nc)

        with ExitStack() as SP:
            E = SP.enter_context
            ACC = E(nc.sbuf_tensor("ACC", [128, 8, D], F32))
            x2s = E(nc.sbuf_tensor("x2s", [128, SC, TG], BF16))
            xng = E(nc.sbuf_tensor("xng", [128, KD, TG], BF16))
            wdn = [[E(nc.sbuf_tensor(f"wdn{p}_{i}", [128, D], BF16)) for i in range(SC)]
                   for p in range(2)]
            s_stage = [E(nc.sbuf_tensor(f"sstage{i}", [128, D], BF16)) for i in range(2)]
            upq = E(nc.sbuf_tensor("upq", [128, KD, 128], I32))
            wupbf = [E(nc.sbuf_tensor(f"wupbf{i}", [128, KD, 128], BF16)) for i in range(2)]
            la_sb = E(nc.sbuf_tensor("lasb", [16, TG], BF16))
            laBs = E(nc.sbuf_tensor("laBs", [16, 1024], BF16))
            ldb = la_sb
            laA_sb = E(nc.sbuf_tensor("laAsb", [128, KD, R], BF16))
            ldA_sb = [E(nc.sbuf_tensor(f"ldAsb{i}", [128, SC, R], BF16)) for i in range(2)]
            Bd_sb = E(nc.sbuf_tensor("Bdsb", [16, 1024], BF16))
            bup_sb = E(nc.sbuf_tensor("bupsb", [128, KH], F32))
            ps_up = [E(nc.psum_tensor(f"psup{i}", [128, 512], F32)) for i in range(2)]
            pdown = [E(nc.psum_tensor(f"pdn{i}", [128, 1024], F32)) for i in range(2)]
            ps_ld = E(nc.psum_tensor("psld", [16, 1024], F32))

            xng_toks0 = [None] * 8
            for tc8 in range(8):
                wt("y", "pe", p34_pe)
                wtt("y", xnw_toks[tc8])
                xng_toks0[tc8] = op("y", lambda e, tc8=tc8: e.dma_start(
                    xng[:, :, tc8 * 128:(tc8 + 1) * 128], xnT_scr[tc8]), "hw", 16)
            wts("g", attn_wr_done)
            wts("y", attn_wr_done)
            wts("v", attn_wr_done)
            cl1 = op("g", lambda e: e.dma_start(laA_sb[:], laA_d), "wl", 16)
            cl4 = op("y", lambda e: e.dma_start(bup_sb[:], bup_d), "hw", 16)

            out_dma = [[None] * 8 for _ in range(NG)]
            pe_x2_free = 0
            wdn_free = [0, 0]
            dq_next = None
            pdown_free = [("dv", 0), ("dv", 0)]
            psup_free = [("ac", 0), ("ac", 0)]
            s_slot_user = [0, 0]
            s_use = 0
            upq_dv_free = [0, 0]
            wupbf_pe_free = [0, 0]
            psld_free = 0
            ldA_pe_free = [0, 0]
            laBs_pe_free = 0
            ld_mm_pe = 0
            ldb_pe_free = 0
            la_pe_free = 0

            xng_toks_next = None
            pre_us = {}
            for cj in range(2):
                slotj = s_use % 2
                s_use += 1
                pre_us[cj] = (op("g", lambda e, cj=cj, slotj=slotj: e.dma_start(
                    s_stage[slotj][:].rearrange("p (k c) -> p k c", k=KD),
                    wups_d[cj]), "wl", 16), slotj)
            dq_next = []
            for ci in range(SC):
                dq_next.append(op("g", lambda e, ci=ci: e.dma_start(
                    wdn[0][ci][:], wdnq_d[ci]), "wl", 16))

            for gi in range(NG):
                gcol = gi * TG
                xng_toks = xng_toks0 if gi == 0 else xng_toks_next
                la_ev = 0
                for tb2 in range(2):
                    wtt("t", cl1)
                    wts("t", [t for t in xng_toks[tb2 * 4:(tb2 + 1) * 4] if t])
                    bk, bv = psup_free[tb2]
                    wt("t", bk, bv)
                    mmv = None
                    for k in range(KD):
                        inc = "pe" if k == KD - 1 else None
                        mmv = op("t", lambda e, k=k, tb2=tb2: e.matmul(
                            ps_up[tb2][0:16, :], laA_sb[:, k, :],
                            xng[:, k, tb2 * 512:(tb2 + 1) * 512],
                            start=(k == 0), stop=(k == KD - 1)), inc) or mmv
                    wt("v", "pe", mmv)
                    if gi > 0:
                        wt("v", "pe", la_pe_free)
                    la_ev = op("v", lambda e, tb2=tb2: e.tensor_copy(
                        la_sb[:, tb2 * 512:(tb2 + 1) * 512], ps_up[tb2][0:16, :]), "dv")
                    psup_free[tb2] = ("dv", la_ev)

                for s5 in range(NS):
                    S = gi * NS + s5
                    par = S % 2
                    dq_lds = dq_next
                    ds_toks = [None] * SC
                    ds_slots = [0] * SC
                    deq_dn_arr = [0] * SC
                    wt("g", "pe", laBs_pe_free)
                    lb_ld = op("g", lambda e, s5=s5: e.dma_start(
                        laBs[:], laB_d[:, s5 * 1024:(s5 + 1) * 1024]), "wl", 16)
                    sp_ = s5 % 2
                    wt("g", "pe", ldA_pe_free[sp_])
                    ldA_ld = op("g", lambda e, s5=s5, sp_=sp_: e.dma_start(
                        ldA_sb[sp_][:], ldA_d[:, s5 * SC:(s5 + 1) * SC, :]), "wl", 16)
                    relu_ev = 0
                    for ci in range(SC):
                        c = s5 * SC + ci
                        wp = ci % 2
                        if s5 == 0 and ci in pre_us:
                            us_tok, slot = pre_us.pop(ci)
                        else:
                            slot = s_use % 2
                            s_use += 1
                            wt("g", "dv", s_slot_user[slot])
                            us_tok = op("g", lambda e, c=c, slot=slot: e.dma_start(
                                s_stage[slot][:].rearrange("p (k c) -> p k c", k=KD),
                                wups_d[c]), "wl", 16)
                        KH2 = KD // 2
                        dqh = [0, 0]
                        for hk in range(2):
                            wt("y", "dv", upq_dv_free[hk])
                            uq_ld = op("y", lambda e, c=c, hk=hk, KH2=KH2: e.dma_start(
                                upq[:, hk * KH2:(hk + 1) * KH2, :],
                                wupq_d[c][:, hk * KH2:(hk + 1) * KH2, :]), "hw", 16)
                            wtt("v", uq_ld)
                            if hk == 0:
                                wtt("v", us_tok)
                                wt("v", "pe", wupbf_pe_free[wp])
                            dqh[hk] = op("v", lambda e, wp=wp, slot=slot, hk=hk, KH2=KH2: e.scalar_tensor_tensor(
                                wupbf[wp][:, hk * KH2:(hk + 1) * KH2, :],
                                upq[:, hk * KH2:(hk + 1) * KH2, :], -7.5,
                                s_stage[slot][:].rearrange("p (k c) -> p k c", k=KD)[:, hk * KH2:(hk + 1) * KH2, :],
                                ALU.add, ALU.mult), "dv")
                            upq_dv_free[hk] = dqh[hk]
                        dq = dqh[1]
                        s_slot_user[slot] = dq
                        djs = [ci - 1] if ci >= 1 else []
                        if ci == SC - 1:
                            djs += [SC - 1]
                        for dj in djs:
                            slotd = s_use % 2
                            s_use += 1
                            wt("g", "dv", s_slot_user[slotd])
                            ds_toks[dj] = op("g", lambda e, dj=dj, slotd=slotd, s5=s5: e.dma_start(
                                s_stage[slotd][:], wdns_d[s5 * SC + dj]), "wl", 16)
                            ds_slots[dj] = slotd
                            wtt("v", dq_lds[dj])
                            wtt("v", ds_toks[dj])
                            deq_dn_arr[dj] = op("v", lambda e, dj=dj, slotd=slotd, par=par: e.scalar_tensor_tensor(
                                wdn[par][dj][:], wdn[par][dj][:], -7.5, s_stage[slotd][:],
                                ALU.add, ALU.mult), "dv")
                            s_slot_user[slotd] = deq_dn_arr[dj]
                        for tb2 in range(2):
                            if tb2 == 0:
                                wt("t", "dv", dqh[0])
                            wtt("t", lb_ld)
                            wt("t", "dv", la_ev)
                            bk, bv = psup_free[tb2]
                            wt("t", bk, bv)
                            if pe_x2_free:
                                wt("s", "pe", pe_x2_free)
                            for k in range(KD):
                                if tb2 == 0 and k == KD // 2:
                                    wt("t", "dv", dqh[1])
                                op("t", lambda e, wp=wp, k=k, tb2=tb2: e.matmul(
                                    ps_up[tb2][:], wupbf[wp][:, k, :],
                                    xng[:, k, tb2 * 512:(tb2 + 1) * 512],
                                    start=(k == 0), stop=False))
                            upmm = op("t", lambda e, ci=ci, tb2=tb2: e.matmul(
                                ps_up[tb2][:], laBs[:, ci * 128:(ci + 1) * 128],
                                la_sb[:, tb2 * 512:(tb2 + 1) * 512],
                                start=False, stop=True), "pe")
                            wupbf_pe_free[wp] = upmm
                            laBs_pe_free = upmm
                            la_pe_free = upmm
                            wt("s", "pe", upmm)
                            if gi == 0 and s5 == 0 and ci == 0 and tb2 == 0:
                                wtt("s", cl4)
                            relu_ev = op("s", lambda e, ci=ci, tb2=tb2, c=c: e.activation(
                                x2s[:, ci, tb2 * 512:(tb2 + 1) * 512], ps_up[tb2][:],
                                AF.Relu, bias=bup_sb[:, c:c + 1]), "ac")
                            psup_free[tb2] = ("ac", relu_ev)
                    wt("t", "ac", relu_ev)
                    if s5 == 0:
                        wt("t", "dv", psld_free)
                        wtt("t", ldA_ld)
                    else:
                        wtt("t", ldA_ld)
                    for tb2 in range(2):
                        for ci in range(SC):
                            inc = "pe" if (ci == SC - 1 and tb2 == 1) else None
                            r = op("t", lambda e, sp_=sp_, ci=ci, tb2=tb2, s5=s5: e.matmul(
                                ps_ld[0:16, tb2 * 512:(tb2 + 1) * 512], ldA_sb[sp_][:, ci, :],
                                x2s[:, ci, tb2 * 512:(tb2 + 1) * 512],
                                start=(s5 == 0 and ci == 0), stop=(ci == SC - 1)), inc)
                            if inc:
                                ld_mm_pe = r
                    ldA_pe_free[sp_] = ld_mm_pe
                    if S + 1 < NG * NS:
                        npar = (S + 1) % 2
                        dq_next = []
                        for ci in range(SC):
                            c2 = ((S + 1) % NS) * SC + ci
                            wt("g", "pe", wdn_free[npar])
                            dq_next.append(op("g", lambda e, c2=c2, ci=ci, npar=npar: e.dma_start(
                                wdn[npar][ci][:], wdnq_d[c2]), "wl", 16))
                    if s5 == 0:
                        acc_init = [None] * 8
                        for tcg in range(8):
                            if gi > 0:
                                wtt("g", out_dma[gi - 1][tcg])
                            acc_init[tcg] = op("g", lambda e, tcg=tcg, gcol=gcol: e.dma_start(
                                ACC[:, tcg, :], attn_scr[gcol + tcg * 128:gcol + (tcg + 1) * 128, :]), "wl", 16)
                    if s5 == NS - 1 and gi + 1 < NG:
                        xng_toks_next = [None] * 8
                        for tc8 in range(8):
                            tc = (gi + 1) * 8 + tc8
                            wt("y", "pe", la_pe_free)
                            xng_toks_next[tc8] = op("y", lambda e, tc=tc, tc8=tc8: e.dma_start(
                                xng[:, :, tc8 * 128:(tc8 + 1) * 128], xnT_scr[tc]), "hw", 16)
                        for cj in range(2):
                            slotj = s_use % 2
                            s_use += 1
                            wt("g", "dv", s_slot_user[slotj])
                            pre_us[cj] = (op("g", lambda e, cj=cj, slotj=slotj: e.dma_start(
                                s_stage[slotj][:].rearrange("p (k c) -> p k c", k=KD),
                                wups_d[cj]), "wl", 16), slotj)
                    relu_all = cnt["ac"]
                    dn_last = 0
                    for hf in range(2):
                        for tcg in range(8):
                            dp = (hf * 8 + tcg) % 2
                            wt("t", "ac", relu_all)
                            bk, bv = pdown_free[dp]
                            wt("t", bk, bv)
                            for ci in range(SC):
                                wt("t", "dv", deq_dn_arr[ci])
                                for nb2 in range(2):
                                    inc = "pe" if (ci == SC - 1 and nb2 == 1) else None
                                    r = op("t", lambda e, dp=dp, ci=ci, nb2=nb2, tcg=tcg, hf=hf, par=par: e.matmul(
                                        pdown[dp][:, nb2 * 512:(nb2 + 1) * 512],
                                        x2s[:, ci, tcg * 128:(tcg + 1) * 128],
                                        wdn[par][ci][:, hf * 1024 + nb2 * 512:hf * 1024 + (nb2 + 1) * 512],
                                        start=(ci == 0), stop=(ci == SC - 1 and nb2 == 1)), inc)
                                    if inc:
                                        dn_mm = r
                            dn_last = dn_mm
                            wt("v", "pe", dn_mm)
                            wtt("v", acc_init[tcg])
                            aad = op("v", lambda e, tcg=tcg, hf=hf, dp=dp: e.tensor_tensor(
                                ACC[:, tcg, hf * 1024:(hf + 1) * 1024],
                                ACC[:, tcg, hf * 1024:(hf + 1) * 1024],
                                pdown[dp][:], ALU.add), "dv")
                            pdown_free[dp] = ("dv", aad)
                    pe_x2_free = dn_last
                    wdn_free[par] = dn_last
                wt("v", "pe", ld_mm_pe)
                wt("v", "pe", la_pe_free)
                if gi > 0:
                    wt("v", "pe", ldb_pe_free)
                ldb_ev = op("v", lambda e: e.tensor_copy(ldb[:], ps_ld[0:16, :]), "dv")
                psld_free = ldb_ev
                if dbg and gi == 0:
                    wt("g", "ac", cnt["ac"])
                    op("g", lambda e: e.dma_start(dbg_x2, x2s[:]), "wl", 16)
                bd_adds = [[0, 0] for _ in range(8)]
                bd_prev_mm = ldb_pe_free
                for jh in range(2):
                    wt("g", "pe", bd_prev_mm)
                    bdld = op("g", lambda e, jh=jh: e.dma_start(
                        Bd_sb[:], ldB_d[:, jh * 1024:(jh + 1) * 1024]), "wl", 16)
                    for tcg in range(8):
                        dp = (jh * 8 + tcg) % 2
                        wt("t", "dv", ldb_ev)
                        wtt("t", bdld)
                        bk, bv = pdown_free[dp]
                        wt("t", bk, bv)
                        op("t", lambda e, dp=dp, tcg=tcg: e.matmul(
                            pdown[dp][:, 0:512], ldb[:, tcg * 128:(tcg + 1) * 128],
                            Bd_sb[:, 0:512], start=True, stop=False))
                        bd_mm = op("t", lambda e, dp=dp, tcg=tcg: e.matmul(
                            pdown[dp][:, 512:1024], ldb[:, tcg * 128:(tcg + 1) * 128],
                            Bd_sb[:, 512:1024], start=True, stop=True), "pe")
                        ldb_pe_free = bd_mm
                        wt("v", "pe", bd_mm)
                        bd_adds[tcg][jh] = op("v", lambda e, tcg=tcg, jh=jh, dp=dp: e.tensor_tensor(
                            ACC[:, tcg, jh * 1024:(jh + 1) * 1024],
                            ACC[:, tcg, jh * 1024:(jh + 1) * 1024],
                            pdown[dp][:], ALU.add), "dv")
                        pdown_free[dp] = ("dv", bd_adds[tcg][jh])
                    bd_prev_mm = ldb_pe_free
                for tcg in range(8):
                    wt("g", "dv", max(bd_adds[tcg]))
                    xa = op("g", lambda e, tcg=tcg, gcol=gcol: e.dma_start(
                        ACC[:, tcg, :], x_d[gcol + tcg * 128:gcol + (tcg + 1) * 128, :],
                        accum_op=ALU.add), "wl", 16)
                    wtt("y", xa)
                    out_dma[gi][tcg] = op("y", lambda e, tcg=tcg, gcol=gcol: e.dma_start(
                        out_d[gcol + tcg * 128:gcol + (tcg + 1) * 128, :],
                        ACC[:, tcg, :]), "out", 16)
            if dbg:
                wts("g", fence("wl"))
            wts("y", fence("out"))
            B.run(nc)

    return nc


def _prep_inputs(L0_attn_Wqkv, L0_attn_Wg, L0_attn_Wout, L0_attn_norm_w,
                 L0_ffn_w_up_q, L0_ffn_w_up_scale, L0_ffn_b_up,
                 L0_ffn_w_up_lora_a, L0_ffn_w_up_lora_b,
                 L0_ffn_w_down_q, L0_ffn_w_down_scale, L0_ffn_b_down,
                 L0_ffn_w_down_lora_a, L0_ffn_w_down_lora_b):
    C = np.ascontiguousarray
    f = lambda a: np.asarray(a, np.float32)
    wqkv_p = C(f(L0_attn_Wqkv).reshape(D, H, 384).transpose(1, 0, 2)
               .reshape(H, KD, 128, 384).transpose(0, 2, 1, 3))
    wg_p = C(f(L0_attn_Wg).reshape(KD, 128, H, 128).transpose(2, 1, 0, 3))
    wout_p = C(f(L0_attn_Wout).reshape(KD, 128, 4, 512).transpose(2, 1, 0, 3))
    nw_p = C(f(L0_attn_norm_w).reshape(H, 128).T)
    wupq_p = C(np.asarray(L0_ffn_w_up_q, np.int32).T
               .reshape(KD, 128, KH, 128).transpose(2, 1, 0, 3))
    wups_p = C(np.repeat(f(L0_ffn_w_up_scale), 64, axis=1).T
               .reshape(KD, 128, KH, 128).transpose(2, 1, 0, 3))
    bup_p = C(f(L0_ffn_b_up).reshape(KH, 128).T)
    laA_p = C(f(L0_ffn_w_up_lora_a).reshape(KD, 128, R).transpose(1, 0, 2))
    laB_p = C(f(L0_ffn_w_up_lora_b))
    wdnq_p = C(np.asarray(L0_ffn_w_down_q, np.int32).T.reshape(KH, 128, D))
    wdns_p = C(np.repeat(f(L0_ffn_w_down_scale), 64, axis=1).T.reshape(KH, 128, D))
    bdn_p = C(f(L0_ffn_b_down).reshape(1, D))
    ldA_p = C(f(L0_ffn_w_down_lora_a).reshape(KH, 128, R).transpose(1, 0, 2))
    ldB_p = C(f(L0_ffn_w_down_lora_b))
    return dict(wqkv=wqkv_p, wg=wg_p, wout=wout_p, nw=nw_p, wupq=wupq_p,
                wups=wups_p, bup=bup_p, laA=laA_p, laB=laB_p, wdnq=wdnq_p,
                wdns=wdns_p, bdn=bdn_p, ldA=ldA_p, ldB=ldB_p)


_NC_CACHE = {}


def kernel(L0_attn_x, L0_attn_Wqkv, L0_attn_Wg, L0_attn_Wout, L0_attn_norm_w,
           L0_ffn_w_up_q, L0_ffn_w_up_scale, L0_ffn_b_up, L0_ffn_w_up_lora_a,
           L0_ffn_w_up_lora_b, L0_ffn_w_down_q, L0_ffn_w_down_scale,
           L0_ffn_b_down, L0_ffn_w_down_lora_a, L0_ffn_w_down_lora_b):
    if "nc" not in _NC_CACHE:
        _NC_CACHE["nc"] = build_nc(dbg=False)
    nc = _NC_CACHE["nc"]
    shared = _prep_inputs(L0_attn_Wqkv, L0_attn_Wg, L0_attn_Wout, L0_attn_norm_w,
                          L0_ffn_w_up_q, L0_ffn_w_up_scale, L0_ffn_b_up,
                          L0_ffn_w_up_lora_a, L0_ffn_w_up_lora_b,
                          L0_ffn_w_down_q, L0_ffn_w_down_scale, L0_ffn_b_down,
                          L0_ffn_w_down_lora_a, L0_ffn_w_down_lora_b)
    x = np.asarray(L0_attn_x, np.float32)
    in_maps = []
    for i in range(8):
        m = dict(shared)
        m["x"] = np.ascontiguousarray(x[i])
        in_maps.append(m)
    res = run_bass_kernel_spmd(nc, in_maps, core_ids=list(range(8)))
    return np.stack([r["out"] for r in res.results], axis=0)
